# version 14
# baseline (speedup 1.0000x reference)
import contextlib
import numpy as np
import ml_dtypes
import concourse.bass as bass
import concourse.mybir as mybir
from concourse.bass_utils import run_bass_kernel_spmd
from concourse.alu_op_type import AluOpType
from concourse.aot_env import allow_ici_all_to_all

F32 = mybir.dt.float32
BF16 = mybir.dt.bfloat16
AF = mybir.ActivationFunctionType
AX = mybir.AxisListType.X

NCORES = 8
D = 4096
L = 8192
HD = 128
HPC = 4
TPC = 1024
NFC = 32
I_TRUE = 11008
ISH = 1408
IPAD = 11264
NICL = 11
WTOK = 512
NWIN = L // WTOK
QC = 512
NQC = L // QC
EPS = 1e-5
ISQHD = 0.08838834764831845
WMEAN_N = float(I_TRUE * D)

ENGS = ("sync", "scalar", "vector", "tensor", "gpsimd")


class Sem:
    def __init__(self, h):
        self.h = h
        self.v = 0

    def bump(self, n):
        self.v += n
        return self.v


class Em:
    def __init__(self):
        self.tape = {k: [] for k in ENGS}

    def op(self, eng, fn, sem=None, inc=1):
        if sem is None:
            self.tape[eng].append(fn)
            return None
        v = sem.bump(inc)
        self.tape[eng].append(lambda e, fn=fn, h=sem.h, i=inc: fn(e).then_inc(h, i))
        return v

    def wait(self, eng, sem, v):
        if v is None or v <= 0:
            return
        self.tape[eng].append(lambda e, h=sem.h, vv=v: e.wait_ge(h, vv))

    def waitm(self, eng, mark):
        if mark is not None:
            self.wait(eng, mark[0], mark[1])


def build_program(debug=False):
    nc = bass.Bass("TRN2", target_bir_lowering=False)

    xT = nc.dram_tensor("xT", [NWIN * D, 512], F32, kind="ExternalInput").ap()
    x_own = nc.dram_tensor("x_own", [TPC, D], F32, kind="ExternalInput").ap()
    wqT = nc.dram_tensor("wqT", [D, HPC * HD], F32, kind="ExternalInput").ap()
    wkT = nc.dram_tensor("wkT", [D, HPC * HD], F32, kind="ExternalInput").ap()
    wvT = nc.dram_tensor("wvT", [D, HPC * HD], F32, kind="ExternalInput").ap()
    bqk = nc.dram_tensor("bqk", [128, 8], F32, kind="ExternalInput").ap()
    bv_row = nc.dram_tensor("bv_row", [1, 512], F32, kind="ExternalInput").ap()
    woT = nc.dram_tensor("woT", [8 * D, 512], F32, kind="ExternalInput").ap()
    bo_row = nc.dram_tensor("bo_row", [1, D], F32, kind="ExternalInput").ap()
    wgT_sh = nc.dram_tensor("wgT_sh", [ISH, D], F32, kind="ExternalInput").ap()
    wuT_sh = nc.dram_tensor("wuT_sh", [ISH, D], F32, kind="ExternalInput").ap()
    wdT_sh = nc.dram_tensor("wdT_sh", [2048, 2816], F32, kind="ExternalInput").ap()
    masks = nc.dram_tensor("masks", [512, 512], BF16, kind="ExternalInput").ap()

    y = nc.dram_tensor("y", [TPC, D], F32, kind="ExternalOutput").ap()

    qT_sc = nc.dram_tensor("qT_sc", [HPC * HD, L], BF16).ap()
    kT_sc = nc.dram_tensor("kT_sc", [HPC * HD, L], BF16).ap()
    v_sc = nc.dram_tensor("v_sc", [HPC * 64 * 128, 128], BF16).ap()
    ao_in = nc.dram_tensor("ao_in", [NCORES * HPC * HD, TPC], BF16).ap()
    ao_t = nc.dram_tensor("ao_t", [D, TPC], BF16).ap()
    wg_qloc = nc.dram_tensor("wg_qloc", [ISH, D], BF16).ap()
    wu_qloc = nc.dram_tensor("wu_qloc", [ISH, D], BF16).ap()
    wd_qloc = nc.dram_tensor("wd_qloc", [2048, 2816], BF16).ap()
    wg_q = nc.dram_tensor("wg_q", [IPAD, D], BF16, addr_space="Shared").ap()
    wu_q = nc.dram_tensor("wu_q", [IPAD, D], BF16, addr_space="Shared").ap()
    wd_q = nc.dram_tensor("wd_q", [16384, 2816], BF16, addr_space="Shared").ap()
    accd = nc.dram_tensor("accd", [3, 128], F32).ap()
    scl_in = nc.dram_tensor("scl_in", [1, 8], F32).ap()
    scl_out = nc.dram_tensor("scl_out", [1, 8], F32).ap()
    zd = nc.dram_tensor("zd", [TPC, D], F32).ap()
    xn32d = nc.dram_tensor("xn32d", [TPC, D], F32).ap()
    xnbfd = nc.dram_tensor("xnbfd", [TPC, D], BF16).ap()
    z2p = nc.dram_tensor("z2p", [TPC, D], F32).ap()
    r2d = nc.dram_tensor("r2d", [TPC, D], F32).ap()

    dbg = {}
    if debug:
        for name, shape, dt in [
            ("d_qT", [HPC * HD, L], BF16), ("d_kT", [HPC * HD, L], BF16),
            ("d_v", [HPC * 64 * 128, 128], BF16), ("d_aot", [D, TPC], BF16),
            ("d_xn", [TPC, D], F32), ("d_wg", [ISH, D], BF16),
            ("d_scl", [1, 8], F32), ("d_cneg", [128, 8], F32),
            ("d_zd", [TPC, D], F32), ("d_aoin", [NCORES * HPC * HD, TPC], BF16),
            ("d_sums", [64, 512], F32), ("d_ps", [128, 1536], F32),
            ("d_can", [512, 1024], BF16),
        ]:
            dbg[name] = nc.dram_tensor(name, shape, dt, kind="ExternalOutput").ap()

    em = Em()

    with contextlib.ExitStack() as ctx:
        A1 = ctx.enter_context(nc.sbuf_tensor("A1", [128, 49152], BF16))
        A2 = ctx.enter_context(nc.sbuf_tensor("A2", [128, 16384], BF16))
        A3 = ctx.enter_context(nc.sbuf_tensor("A3", [128, 16384], BF16))
        F1 = ctx.enter_context(nc.sbuf_tensor("F1", [128, 8192], F32))
        CB = ctx.enter_context(nc.sbuf_tensor("CB", [128, 1537], BF16))
        CF = ctx.enter_context(nc.sbuf_tensor("CF", [128, 1280], F32))
        QB = ctx.enter_context(nc.sbuf_tensor("QB", [128, 1024], BF16))
        pb = [ctx.enter_context(nc.psum_tensor(f"pb{i}", [128, 512], F32))
              for i in range(8)]

        s_lds = Sem(ctx.enter_context(nc.semaphore("lds")))
        s_ldg = Sem(ctx.enter_context(nc.semaphore("ldg")))
        s_ldv = Sem(ctx.enter_context(nc.semaphore("ldv")))
        s_sts = Sem(ctx.enter_context(nc.semaphore("sts")))
        s_stv = Sem(ctx.enter_context(nc.semaphore("stv")))
        s_pe = Sem(ctx.enter_context(nc.semaphore("pe")))
        s_act = Sem(ctx.enter_context(nc.semaphore("act")))
        s_dve = Sem(ctx.enter_context(nc.semaphore("dve")))
        s_cc = [Sem(ctx.enter_context(nc.semaphore(f"cc{i}"))) for i in range(5)]
        s_gate = Sem(ctx.enter_context(nc.semaphore("gate")))
        s_ldx = [Sem(ctx.enter_context(nc.semaphore(f"ldx{i}"))) for i in range(2)]
        s_ldq = [Sem(ctx.enter_context(nc.semaphore(f"ldq{i}"))) for i in range(2)]
        s_ldkv = [Sem(ctx.enter_context(nc.semaphore(f"ldkv{i}"))) for i in range(2)]
        s_ldqt = [Sem(ctx.enter_context(nc.semaphore(f"ldqt{i}"))) for i in range(2)]
        s_ldm = Sem(ctx.enter_context(nc.semaphore("ldm")))
        s_ldw = [Sem(ctx.enter_context(nc.semaphore(f"ldw{i}"))) for i in range(2)]
        s_ldd = [Sem(ctx.enter_context(nc.semaphore(f"ldd{i}"))) for i in range(2)]
        s_ldz = [Sem(ctx.enter_context(nc.semaphore(f"ldz{i}"))) for i in range(2)]
        s_ldr = Sem(ctx.enter_context(nc.semaphore("ldr")))
        s_stg = [Sem(ctx.enter_context(nc.semaphore(f"stg{i}"))) for i in range(3)]
        s_stq = [Sem(ctx.enter_context(nc.semaphore(f"stq{i}"))) for i in range(2)]

        ones_col = CB[:, 0:1]
        stg = [CB[:, 1:513], CB[:, 513:1025], CB[:, 1025:1537]]
        ones_row_f = CF[0:1, 1:129]
        bqk_sb = CF[:, 136:144]
        cneg_sb = CF[:, 144:148]
        t_ap = CF[:, 156:159]
        negt_ap = CF[:, 159:162]
        stot_ap = CF[:, 162:163]
        ssum_sb = CF[0:1, 256:768]
        rs_sb = CF[0:1, 768:1280]
        bc_sb = F1[:, 2048:2560]
        bv_bc = F1[:, 5632:6144]

        bank_free = [None] * 8

        def mm_group(bank_i, emits):
            em.waitm("tensor", bank_free[bank_i])
            v = None
            for i, fn in enumerate(emits):
                if i == len(emits) - 1:
                    v = em.op("tensor", fn, s_pe, 1)
                else:
                    em.op("tensor", fn)
            return (s_pe, v)

        if debug:
            em.op("sync", lambda e: e.dma_start(out=ao_in[2048:2560, 0:512],
                                                in_=masks[:, :]), s_sts, 16)
        em.op("sync", lambda e: e.dma_start(out=bqk_sb, in_=bqk[:, :]), s_lds, 16)
        p0_lds = s_lds.v
        bvl = em.op("sync", lambda e: e.dma_start(out=CF[0:1, 256:768],
                                                  in_=bv_row[:, :]), s_lds, 16)
        em.op("vector", lambda e: e.memset(ones_col, 1.0))
        em.op("vector", lambda e: e.memset(CF[0:1, 1:129], 1.0))
        em.op("vector", lambda e: e.memset(CF[0:1, 148:156], 0.0), s_dve, 1)
        consts_dve = s_dve.v
        em.wait("tensor", s_lds, bvl)
        em.wait("tensor", s_dve, consts_dve)
        pv = em.op("tensor", lambda e: e.matmul(pb[7][:, :], ones_row_f,
                                                CF[0:1, 256:768], start=True,
                                                stop=True), s_pe, 1)
        em.wait("scalar", s_pe, pv)
        av = em.op("scalar", lambda e: e.copy(bv_bc, pb[7][:, :]), s_act, 1)
        bank_free[7] = (s_act, av)
        bvbc_done = (s_act, av)

        em.op("vector", lambda e: e.memset(CF[:, 384:480], 0.0))
        wsrcs = [(wgT_sh, ISH, D), (wuT_sh, ISH, D), (wdT_sh, 2048, 2816)]
        p1_slots = [F1[:, 0:2048], F1[:, 2048:4096]]
        p1_pieces = []
        for wi, (src, rows, cols) in enumerate(wsrcs):
            widths = [2048] * (cols // 2048) + ([cols % 2048] if cols % 2048 else [])
            for ti in range(rows // 128):
                off = 0
                for wdt in widths:
                    p1_pieces.append((wi, src, ti, off, wdt))
                    off += wdt
        p1_i = 0
        p1_slot_mark = [None, None]

        def emit_p1_piece():
            nonlocal p1_i
            if p1_i >= len(p1_pieces):
                return
            wi, src, ti, off, wdt = p1_pieces[p1_i]
            sl = p1_i % 2
            em.waitm("sync", p1_slot_mark[sl])
            lv = em.op("sync",
                       lambda e, sl=sl, src=src, ti=ti, off=off, wdt=wdt:
                       e.dma_start(out=p1_slots[sl][:, 0:wdt],
                                   in_=src[ti * 128:(ti + 1) * 128,
                                           off:off + wdt]), s_ldq[sl], 16)
            em.wait("vector", s_ldq[sl], lv)
            col = 384 + p1_i
            dv = em.op("vector", lambda e, sl=sl, wdt=wdt, col=col:
                       e.tensor_reduce(CF[:, col:col + 1],
                                       p1_slots[sl][:, 0:wdt],
                                       axis=AX, op=AluOpType.add,
                                       apply_absolute_value=True), s_dve, 1)
            p1_slot_mark[sl] = (s_dve, dv)
            p1_i += 1

        def emit_p1_finalize():
            bounds = []
            n0 = 0
            for wsrc_, rows_, cols_ in wsrcs:
                n_p = (rows_ // 128) * ((cols_ + 2047) // 2048)
                bounds.append((n0, n0 + n_p))
                n0 += n_p
            em.wait("vector", s_dve, s_dve.v)
            dvv = None
            for wi in range(3):
                b0, b1 = bounds[wi]
                dvv = em.op("vector", lambda e, wi=wi, b0=b0, b1=b1:
                            e.tensor_reduce(CF[:, 163 + wi:164 + wi],
                                            CF[:, 384 + b0:384 + b1],
                                            axis=AX, op=AluOpType.add),
                            s_dve, 1)
            em.wait("gpsimd", s_dve, dvv)
            for wi in range(3):
                em.op("gpsimd", lambda e, wi=wi:
                      e.dma_start(out=accd[wi:wi + 1, :],
                                  in_=CF[:, 163 + wi:164 + wi]), s_stv, 16)
            em.wait("gpsimd", s_stv, s_stv.v)
            lv = em.op("gpsimd", lambda e:
                       e.dma_start(out=CF[0:1, 896:1280],
                                   in_=accd[:, :]), s_ldg, 16)
            em.wait("gpsimd", s_ldg, lv)
            gv = em.op("gpsimd", lambda e: e.nop(), s_gate, 1)
            em.wait("vector", s_gate, gv)
            for wi in range(3):
                em.op("vector", lambda e, wi=wi:
                      e.tensor_reduce(CF[0:1, 168 + wi:169 + wi],
                                      CF[0:1, 896 + wi * 128:1024 + wi * 128],
                                      axis=AX, op=AluOpType.add))
            em.op("vector", lambda e: e.memset(CF[0:1, 171:176], 0.0), s_dve, 1)
            sder = s_dve.v
            em.wait("gpsimd", s_dve, sder)
            sv = em.op("gpsimd", lambda e: e.dma_start(out=scl_in[:, :],
                                                       in_=CF[0:1, 168:176]),
                       s_stv, 16)
            em.wait("gpsimd", s_stv, sv)
            em.op("gpsimd", lambda e: e.collective_compute(
                "AllReduce", AluOpType.add, replica_groups=[list(range(NCORES))],
                ins=[scl_in[:, :]], outs=[scl_out[:, :]]), s_cc[0], 1)
            em.wait("gpsimd", s_cc[0], 1)
            lv = em.op("gpsimd", lambda e: e.dma_start(out=CF[0:1, 176:184],
                                                       in_=scl_out[:, :]),
                       s_ldg, 16)
            em.wait("gpsimd", s_ldg, lv)
            gv = em.op("gpsimd", lambda e: e.nop(), s_gate, 1)
            em.wait("vector", s_gate, gv)
            v_ = em.op("vector", lambda e: e.tensor_scalar(CF[0:1, 184:187],
                                                           CF[0:1, 176:179],
                                                           1.0 / WMEAN_N, None,
                                                           AluOpType.mult,
                                                           AluOpType.bypass),
                       s_dve, 1)
            em.wait("vector", s_dve, v_)
            em.op("vector", lambda e: e.tensor_scalar(CF[0:1, 187:190],
                                                      CF[0:1, 184:187],
                                                      EPS, 0.5, AluOpType.add,
                                                      AluOpType.mult))
            v_ = em.op("vector", lambda e: e.tensor_tensor(CF[0:1, 191:192],
                                                           CF[0:1, 184:185],
                                                           CF[0:1, 185:186],
                                                           AluOpType.mult),
                       s_dve, 1)
            em.wait("vector", s_dve, v_)
            v_ = em.op("vector", lambda e: e.tensor_tensor(CF[0:1, 190:191],
                                                           CF[0:1, 191:192],
                                                           CF[0:1, 186:187],
                                                           AluOpType.mult),
                       s_dve, 1)
            em.wait("vector", s_dve, v_)
            dv = em.op("vector", lambda e:
                       e.tensor_scalar(CF[0:1, 192:193], CF[0:1, 190:191], -1.0,
                                       None, AluOpType.mult, AluOpType.bypass),
                       s_dve, 1)
            return (s_dve, dv)

        def emit_scale_broadcasts(scal_mark):
            em.waitm("tensor", scal_mark)
            for k in range(4):
                src_col = 187 + k if k < 3 else 192
                em.waitm("tensor", bank_free[7])
                pv = em.op("tensor", lambda e, src_col=src_col:
                           e.matmul(pb[7][:, 0:1], ones_row_f,
                                    CF[0:1, src_col:src_col + 1], start=True,
                                    stop=True), s_pe, 1)
                em.wait("scalar", s_pe, pv)
                if k < 3:
                    av = em.op("scalar", lambda e, k=k:
                               e.copy(t_ap[:, k:k + 1], pb[7][:, 0:1]),
                               s_act, 1)
                else:
                    av = em.op("scalar", lambda e: e.copy(stot_ap,
                                                          pb[7][:, 0:1]),
                               s_act, 1)
                bank_free[7] = (s_act, av)
            em.wait("vector", s_act, bank_free[7][1])
            dv = em.op("vector", lambda e:
                       e.tensor_scalar(negt_ap, t_ap, -1.0, None,
                                       AluOpType.mult, AluOpType.bypass),
                       s_dve, 1)
            return (s_dve, dv)

        A1q, A1k, A1v = A1[:, 0:16384], A1[:, 16384:32768], A1[:, 32768:49152]
        wst = [F1[:, 4096:4608], F1[:, 4608:5120]]
        wst_mark = [None, None]
        wc = 0
        wload_dve = None
        for wsrc, adst in ((wqT, A1q), (wkT, A1k), (wvT, A1v)):
            for fc in range(NFC):
                sl_ = wc % 2
                em.waitm("sync", wst_mark[sl_])
                lw = em.op("sync", lambda e, wsrc=wsrc, fc=fc, sl_=sl_:
                           e.dma_start(out=wst[sl_],
                                       in_=wsrc[fc * 128:(fc + 1) * 128, :]),
                           s_ldq[sl_], 16)
                em.wait("vector", s_ldq[sl_], lw)
                wload_dve = em.op("vector", lambda e, adst=adst, fc=fc, sl_=sl_:
                                  e.tensor_copy(adst[:, fc * 512:(fc + 1) * 512],
                                                wst[sl_]), s_dve, 1)
                wst_mark[sl_] = (s_dve, wload_dve)
                wc += 1

        quant_pieces = []
        for wi, (qsrc, qdst, rows, cols) in enumerate([
                (wgT_sh, wg_qloc, ISH, D), (wuT_sh, wu_qloc, ISH, D),
                (wdT_sh, wd_qloc, 2048, 2816)]):
            for ti in range(rows // 128):
                off = 0
                widths = [512] * (cols // 512) + ([cols % 512] if cols % 512 else [])
                for wdt in widths:
                    quant_pieces.append((wi, qsrc, qdst, ti, off, wdt))
                    off += wdt
        qp_i = 0
        quant_ready = [None]
        q_slotmark = [None, None]
        qb_slotmark = [None, None]
        q_wslots = [F1[:, 4096:4608], F1[:, 4608:5120]]
        q_aslot = F1[:, 5120:5632]
        quant_weight_done = [None, None, None]

        def emit_quant_piece():
            nonlocal qp_i
            if qp_i >= len(quant_pieces):
                return
            wi, qsrc, qdst, ti, off, wdt = quant_pieces[qp_i]
            sl = qp_i % 2
            if qp_i == 0:
                em.waitm("vector", quant_ready[0])
                em.waitm("sync", quant_ready[0])
            em.waitm("vector", q_slotmark[sl])
            wslot = q_wslots[sl]
            em.waitm("sync", q_slotmark[sl])
            lv = em.op("sync",
                       lambda e, qsrc=qsrc, ti=ti, off=off, wdt=wdt, wslot=wslot:
                       e.dma_start(out=wslot[:, 0:wdt],
                                   in_=qsrc[ti * 128:(ti + 1) * 128, off:off + wdt]),
                       s_ldq[sl], 16)
            em.wait("vector", s_ldq[sl], lv)
            v_ = em.op("vector", lambda e, wslot=wslot, wdt=wdt, wi=wi:
                       e.tensor_scalar(q_aslot[:, 0:wdt], wslot[:, 0:wdt],
                                       t_ap[:, wi:wi + 1], None, AluOpType.is_gt,
                                       AluOpType.bypass), s_dve, 1)
            em.wait("vector", s_dve, v_)
            em.waitm("vector", qb_slotmark[sl])
            qb = QB[:, sl * 512:(sl + 1) * 512]
            dv = em.op("vector", lambda e, wslot=wslot, wdt=wdt, wi=wi, qb=qb:
                       e.scalar_tensor_tensor(qb[:, 0:wdt], wslot[:, 0:wdt],
                                              negt_ap[:, wi:wi + 1],
                                              q_aslot[:, 0:wdt],
                                              AluOpType.is_lt, AluOpType.subtract),
                       s_dve, 1)
            q_slotmark[sl] = (s_dve, dv)
            em.wait("sync", s_dve, dv)
            stv = em.op("sync",
                        lambda e, qdst=qdst, ti=ti, off=off, wdt=wdt, qb=qb:
                        e.dma_start(out=qdst[ti * 128:(ti + 1) * 128, off:off + wdt],
                                    in_=qb[:, 0:wdt]), s_stq[sl], 16)
            qb_slotmark[sl] = (s_stq[sl], stv)
            qp_i += 1
            if qp_i == len(quant_pieces) or quant_pieces[qp_i][0] != wi:
                quant_weight_done[wi] = [(s_stq[0], s_stq[0].v),
                                         (s_stq[1], s_stq[1].v)]

        win_done = [None] * NWIN
        stg_mark = [None, None, None]

        for w in range(NWIN):
            xbuf = A2 if w % 2 == 0 else A3
            if w >= 2:
                em.waitm("gpsimd", win_done[w - 2])
            for fc in range(NFC):
                em.op("gpsimd", lambda e, xbuf=xbuf, fc=fc, w=w:
                      e.dma_start(out=xbuf[:, fc * 512:(fc + 1) * 512],
                                  in_=xT[w * D + fc * 128:w * D + (fc + 1) * 128,
                                         :]), s_ldx[w % 2], 16)
            em.wait("tensor", s_ldx[w % 2], s_ldx[w % 2].v)
            if w == 0:
                em.wait("tensor", s_dve, wload_dve)
                em.wait("tensor", s_lds, p0_lds)
                em.wait("tensor", s_dve, consts_dve)
            g = 0
            for h in range(HPC):
                for qk in range(2):
                    bank = g % 4
                    wmat = A1q if qk == 0 else A1k
                    emits = []
                    for fc in range(NFC):
                        emits.append(
                            lambda e, wmat=wmat, fc=fc, h=h, xbuf=xbuf, bank=bank,
                            st=(fc == 0), sp=(fc == NFC - 1):
                            e.matmul(pb[bank][:, :],
                                     wmat[:, fc * 512 + h * 128:
                                          fc * 512 + (h + 1) * 128],
                                     xbuf[:, fc * 512:(fc + 1) * 512],
                                     start=st, stop=sp))
                    mk = mm_group(bank, emits)
                    s2 = g % 2
                    em.waitm("scalar", mk)
                    em.waitm("scalar", stg_mark[s2])
                    av = em.op("scalar", lambda e, bank=bank, s2=s2, h=h, qk=qk:
                               e.activation(stg[s2], pb[bank][:, :], AF.Identity,
                                            bias=bqk_sb[:, 4 * qk + h:
                                                        4 * qk + h + 1]), s_act, 1)
                    bank_free[bank] = (s_act, av)
                    em.waitm("scalar", stg_mark[2])
                    em.wait("scalar", s_act, av)
                    av2 = em.op("scalar", lambda e, s2=s2:
                                e.activation(stg[2], stg[s2], AF.Square), s_act, 1)
                    nb = 6 + g % 2
                    em.waitm("tensor", bank_free[nb])
                    em.wait("tensor", s_act, av2)
                    pv = em.op("tensor", lambda e, nb=nb:
                               e.matmul(pb[nb][0:1, :], ones_col, stg[2],
                                        start=True, stop=True), s_pe, 1)
                    stg_mark[2] = (s_pe, pv)
                    em.wait("vector", s_pe, pv)
                    col = 768 + (4 * qk + h) * 16 + w
                    dv = em.op("vector", lambda e, nb=nb, col=col:
                               e.tensor_reduce(CF[0:1, col:col + 1],
                                               pb[nb][0:1, :], axis=AX,
                                               op=AluOpType.max), s_dve, 1)
                    bank_free[nb] = (s_dve, dv)
                    dst = qT_sc if qk == 0 else kT_sc
                    em.wait("sync", s_act, av)
                    stv = em.op("sync", lambda e, dst=dst, h=h, w=w, s2=s2:
                                e.dma_start(out=dst[h * 128:(h + 1) * 128,
                                                    w * 512:(w + 1) * 512],
                                            in_=stg[s2]), s_stg[s2], 16)
                    stg_mark[s2] = (s_stg[s2], stv)
                    g += 1
            for st4 in range(4):
                bank = 4 + st4 % 2
                emits = []
                for fc in range(NFC):
                    emits.append(lambda e, fc=fc, st4=st4, xbuf=xbuf, bank=bank,
                                 st=(fc == 0), sp=(fc == NFC - 1):
                                 e.matmul(pb[bank][:, :],
                                          xbuf[:, fc * 512 + st4 * 128:
                                               fc * 512 + (st4 + 1) * 128],
                                          A1v[:, fc * 512:(fc + 1) * 512],
                                          start=st, stop=sp))
                mk = mm_group(bank, emits)
                s2 = st4 % 2
                em.waitm("vector", mk)
                em.waitm("vector", stg_mark[s2])
                if w == 0 and st4 == 0:
                    em.waitm("vector", bvbc_done)
                av = em.op("vector", lambda e, bank=bank, s2=s2:
                           e.tensor_tensor(stg[s2], pb[bank][:, :], bv_bc,
                                           AluOpType.add), s_dve, 1)
                bank_free[bank] = (s_dve, av)
                em.wait("sync", s_dve, av)
                jt = w * 4 + st4
                stv = None
                for h in range(HPC):
                    stv = em.op("sync", lambda e, h=h, jt=jt, s2=s2:
                                e.dma_start(
                                    out=v_sc[(h * 64 + jt) * 128:
                                             (h * 64 + jt + 1) * 128, :],
                                    in_=stg[s2][:, h * 128:(h + 1) * 128]),
                                s_stg[s2], 16)
                stg_mark[s2] = (s_stg[s2], stv)
            win_done[w] = (s_pe, s_pe.v)
            if w < 8:
                for _ in range(10):
                    emit_p1_piece()
            elif w == 8:
                while p1_i < len(p1_pieces):
                    emit_p1_piece()
                scal_mark = emit_p1_finalize()
            elif w == 9:
                quant_ready[0] = emit_scale_broadcasts(scal_mark)
        qkv_done = [(s_stg[i], s_stg[i].v) for i in range(3)]

        em.wait("vector", s_dve, s_dve.v)
        for h in range(HPC):
            em.op("vector", lambda e, h=h:
                  e.tensor_reduce(CF[0:1, 196:197],
                                  CF[0:1, 768 + h * 16:768 + h * 16 + 16],
                                  axis=AX, op=AluOpType.max))
            v_ = em.op("vector", lambda e, h=h:
                       e.tensor_reduce(CF[0:1, 197:198],
                                       CF[0:1, 768 + (4 + h) * 16:768 + (4 + h) * 16 + 16],
                                       axis=AX, op=AluOpType.max), s_dve, 1)
            em.wait("vector", s_dve, v_)
            dv = em.op("vector", lambda e:
                       e.tensor_scalar(CF[0:1, 198:199], CF[0:1, 196:197],
                                       CF[0:1, 197:198], 1.0 / 128.0,
                                       AluOpType.mult, AluOpType.mult), s_dve, 1)
            em.wait("scalar", s_dve, dv)
            av_ = em.op("scalar", lambda e: e.activation(CF[0:1, 199:200],
                                                         CF[0:1, 198:199],
                                                         AF.Ln), s_act, 1)
            em.wait("scalar", s_act, av_)
            av = em.op("scalar", lambda e: e.activation(CF[0:1, 200:201],
                                                        CF[0:1, 199:200], AF.Exp,
                                                        scale=0.5), s_act, 1)
            em.wait("vector", s_act, av)
            dv = em.op("vector", lambda e:
                       e.tensor_scalar(CF[0:1, 201:202], CF[0:1, 200:201], -1.0, None,
                                       AluOpType.mult, AluOpType.bypass), s_dve, 1)
            em.waitm("tensor", bank_free[7])
            em.wait("tensor", s_dve, dv)
            pv = em.op("tensor", lambda e: e.matmul(pb[7][:, 0:1], ones_row_f,
                                                    CF[0:1, 201:202], start=True,
                                                    stop=True), s_pe, 1)
            em.wait("scalar", s_pe, pv)
            av = em.op("scalar", lambda e, h=h: e.copy(cneg_sb[:, h:h + 1],
                                                       pb[7][:, 0:1]), s_act, 1)
            bank_free[7] = (s_act, av)
        cneg_done = (s_act, s_act.v)

        if debug:
            for qd in qkv_done:
                em.waitm("sync", qd)
            em.waitm("sync", cneg_done)
            for name, src in [("d_qT", qT_sc), ("d_kT", kT_sc)]:
                for h in range(4):
                    em.op("sync", lambda e, name=name, src=src, h=h:
                          e.dma_start(out=dbg[name][h * 128:(h + 1) * 128, :],
                                      in_=src[h * 128:(h + 1) * 128, :]), s_sts, 16)
            for b in range(16):
                em.op("sync", lambda e, b=b:
                      e.dma_start(out=dbg["d_v"][b * 2048:(b + 1) * 2048, :],
                                  in_=v_sc[b * 2048:(b + 1) * 2048, :]), s_sts, 16)
            em.op("sync", lambda e: e.dma_start(out=dbg["d_cneg"][:, 0:4],
                                                in_=cneg_sb), s_sts, 16)
            em.op("sync", lambda e: e.dma_start(out=dbg["d_cneg"][:, 4:7],
                                                in_=t_ap), s_sts, 16)
            em.wait("sync", s_cc[0], 1)
            lv = em.op("sync", lambda e: e.dma_start(out=CF[0:1, 640:648],
                                                     in_=scl_out[:, :]), s_lds, 16)
            em.wait("sync", s_lds, lv)
            em.op("sync", lambda e: e.dma_start(out=dbg["d_scl"][:, :],
                                                in_=CF[0:1, 640:648]), s_sts, 16)
            em.waitm("sync", (s_stq[0], s_stq[0].v))
            em.waitm("sync", (s_stq[1], s_stq[1].v))
            for ti in range(11):
                em.op("sync", lambda e, ti=ti:
                      e.dma_start(out=dbg["d_wg"][ti * 128:(ti + 1) * 128, :],
                                  in_=wg_qloc[ti * 128:(ti + 1) * 128, :]),
                      s_sts, 16)
            em.op("sync", lambda e: e.dma_start(out=dbg["d_can"][:, 0:512],
                                                in_=ao_in[2048:2560, 0:512]),
                  s_sts, 16)

        em.waitm("sync", win_done[NWIN - 2])
        for dj in range(4):
            em.op("sync", lambda e, dj=dj:
                  e.dma_start(out=A2[:, dj * 512:(dj + 1) * 512],
                              in_=masks[dj * 128:(dj + 1) * 128, :]), s_ldm, 16)
        em.wait("vector", s_ldm, s_ldm.v)
        qbuf = [A2[:, 2048:2560], A2[:, 2560:3072]]
        ptsl = [A3[:, 0:512], A3[:, 512:1024], A3[:, 1024:1536]]
        ptm = [A3[:, 2560 + k * 512:3072 + k * 512] for k in range(4)]
        aosl = [A3[:, 4608:5120], A3[:, 5120:5632]]
        kv_load_mark = [None] * HPC
        kv_free = [None, None]

        def emit_kv_load(h):
            slot = h % 2
            kbuf = A1[:, slot * 16384: slot * 16384 + 8192]
            vbuf = A1[:, slot * 16384 + 8192: slot * 16384 + 16384]
            for qd in qkv_done:
                em.waitm("sync", qd)
            em.waitm("sync", kv_free[slot])
            if h < 2:
                em.waitm("sync", win_done[NWIN - 1])
            em.op("sync", lambda e, h=h, kbuf=kbuf:
                  e.dma_start(out=kbuf, in_=kT_sc[h * 128:(h + 1) * 128, :]),
                  s_ldkv[slot], 16)
            vr = v_sc.rearrange("(j p) m -> p j m", p=128)
            lv = em.op("sync", lambda e, h=h, vbuf=vbuf, vr=vr:
                       e.dma_start(out=vbuf, in_=vr[:, h * 64:(h + 1) * 64, :]),
                       s_ldkv[slot], 16)
            kv_load_mark[h] = (s_ldkv[slot], lv)
            return kbuf, vbuf

        kvbufs = {0: emit_kv_load(0), 1: emit_kv_load(1)}
        pt_mark = [None, None, None]
        ptm_mark = [None, None, None, None]
        ao_bank_mark = [None, None]
        sums_sc_mark = None
        ao_stg_mark = [None, None]
        qload_mark = [None] * NQC
        q_refill = [None, None]

        for h in range(HPC):
            kbuf, vbuf = kvbufs[h]
            em.waitm("tensor", kv_load_mark[h])
            if h == 0:
                em.waitm("tensor", cneg_done)
            em.waitm("sync", q_refill[0])
            lv = em.op("sync", lambda e, h=h:
                       e.dma_start(out=qbuf[0],
                                   in_=qT_sc[h * 128:(h + 1) * 128, 0:512]),
                       s_ldqt[0], 16)
            qload_mark[0] = (s_ldqt[0], lv)
            for qi in range(NQC):
                nt = 4 * qi + 4
                if qi + 1 < NQC:
                    em.waitm("sync", q_refill[(qi + 1) % 2])
                    lv = em.op("sync", lambda e, h=h, qi=qi:
                               e.dma_start(out=qbuf[(qi + 1) % 2],
                                           in_=qT_sc[h * 128:(h + 1) * 128,
                                                     (qi + 1) * 512:(qi + 2) * 512]),
                               s_ldqt[(qi + 1) % 2], 16)
                    qload_mark[qi + 1] = (s_ldqt[(qi + 1) % 2], lv)
                qb = qbuf[qi % 2]
                em.waitm("tensor", qload_mark[qi])
                ao_bank = 4 + qi % 2
                em.waitm("tensor", ao_bank_mark[qi % 2])
                if sums_sc_mark is not None:
                    em.waitm("tensor", sums_sc_mark)
                exp_mark = [None] * nt
                pv_ready = [None] * nt

                def emit_S(j, nt=nt, kbuf=kbuf, qb=qb, h=h, qi=qi,
                           exp_mark=exp_mark, pv_ready=pv_ready):
                    bank = j % 2
                    em.waitm("tensor", bank_free[bank])
                    v = em.op("tensor", lambda e, j=j, bank=bank:
                              e.matmul(pb[bank][:, :],
                                       kbuf[:, j * 128:(j + 1) * 128], qb,
                                       start=True, stop=True), s_pe, 1)
                    sl = j % 3
                    if debug and h == 0 and qi == 0 and j == 0:
                        em.wait("vector", s_pe, v)
                        em.op("vector", lambda e, bank=bank:
                              e.tensor_copy(F1[:, 6144:6656],
                                            pb[bank][:, :]), s_dve, 1)
                        em.op("vector", lambda e, qb=qb:
                              e.tensor_copy(F1[:, 6656:7168], qb), s_dve, 1)
                        em.op("vector", lambda e, kbuf=kbuf:
                              e.tensor_copy(F1[:, 7168:7296],
                                            kbuf[:, 0:128]), s_dve, 1)
                        dsv3 = s_dve.v
                        em.wait("sync", s_dve, dsv3)
                        em.op("sync", lambda e: e.dma_start(
                            out=dbg["d_ps"][:, 0:512], in_=F1[:, 6144:6656]),
                            s_sts, 16)
                        em.op("sync", lambda e: e.dma_start(
                            out=dbg["d_ps"][:, 512:1024], in_=F1[:, 6656:7168]),
                            s_sts, 16)
                        dss = em.op("sync", lambda e: e.dma_start(
                            out=dbg["d_ps"][:, 1024:1152], in_=F1[:, 7168:7296]),
                            s_sts, 16)
                        em.wait("scalar", s_sts, dss)
                    em.waitm("scalar", pt_mark[sl])
                    em.wait("scalar", s_pe, v)
                    av = em.op("scalar", lambda e, bank=bank, sl=sl:
                               e.activation(ptsl[sl], pb[bank][:, :], AF.Exp,
                                            bias=cneg_sb[:, h:h + 1],
                                            scale=ISQHD), s_act, 1)
                    bank_free[bank] = (s_act, av)
                    exp_mark[j] = (s_act, av)
                    if j >= nt - 4:
                        dj = j - (nt - 4)
                        em.waitm("vector", exp_mark[j])
                        em.waitm("vector", ptm_mark[dj])
                        dvv = em.op("vector", lambda e, sl=sl, dj=dj:
                                    e.tensor_tensor(ptm[dj], ptsl[sl],
                                                    A2[:, dj * 512:(dj + 1) * 512],
                                                    AluOpType.mult), s_dve, 1)
                        pv_ready[j] = (s_dve, dvv)
                    else:
                        pv_ready[j] = exp_mark[j]

                def emit_PV(j, nt=nt, vbuf=vbuf, ao_bank=ao_bank,
                            pv_ready=pv_ready):
                    sl = j % 3
                    src = ptm[j - (nt - 4)] if j >= nt - 4 else ptsl[sl]
                    em.waitm("tensor", pv_ready[j])
                    em.op("tensor", lambda e, j=j, src=src, st=(j == 0),
                          sp=(j == nt - 1):
                          e.matmul(pb[ao_bank][:, :],
                                   vbuf[:, j * 128:(j + 1) * 128], src,
                                   start=st, stop=sp))
                    v = em.op("tensor", lambda e, src=src, st=(j == 0),
                              sp=(j == nt - 1):
                              e.matmul(pb[6][0:1, :], ones_col, src,
                                       start=st, stop=sp), s_pe, 1)
                    pt_mark[sl] = (s_pe, v)
                    if j >= nt - 4:
                        ptm_mark[j - (nt - 4)] = (s_pe, v)
                    return v

                last_pv = None
                for j in range(nt):
                    emit_S(j)
                    if j >= 2:
                        last_pv = emit_PV(j - 2)
                q_refill[qi % 2] = (s_pe, s_pe.v)
                last_pv = emit_PV(nt - 2)
                last_pv = emit_PV(nt - 1)
                em.wait("scalar", s_pe, last_pv)
                av = em.op("scalar", lambda e: e.copy(ssum_sb, pb[6][0:1, :]),
                           s_act, 1)
                sums_sc_mark = (s_act, av)
                if debug:
                    em.wait("sync", s_act, av)
                    em.op("sync", lambda e, h=h, qi=qi:
                          e.dma_start(out=dbg["d_sums"][h * 16 + qi:
                                                        h * 16 + qi + 1, :],
                                      in_=ssum_sb), s_sts, 16)
                em.wait("vector", s_act, av)
                dv = em.op("vector", lambda e: e.reciprocal(rs_sb, ssum_sb),
                           s_dve, 1)
                em.waitm("tensor", bank_free[7])
                em.wait("tensor", s_dve, dv)
                pv = em.op("tensor", lambda e: e.matmul(pb[7][:, :], ones_row_f,
                                                        rs_sb, start=True,
                                                        stop=True), s_pe, 1)
                em.wait("scalar", s_pe, pv)
                av = em.op("scalar", lambda e: e.copy(bc_sb, pb[7][:, :]),
                           s_act, 1)
                bank_free[7] = (s_act, av)
                sl2 = qi % 2
                em.waitm("vector", ao_stg_mark[sl2])
                em.wait("vector", s_act, av)
                dv = em.op("vector", lambda e, ao_bank=ao_bank, sl2=sl2:
                           e.tensor_tensor(aosl[sl2], pb[ao_bank][:, :], bc_sb,
                                           AluOpType.mult), s_dve, 1)
                ao_bank_mark[qi % 2] = (s_dve, dv)
                em.wait("sync", s_dve, dv)
                rblk = qi // 2
                stv = em.op("sync", lambda e, h=h, rblk=rblk, qi=qi, sl2=sl2:
                            e.dma_start(
                                out=ao_in[rblk * 512 + h * 128:
                                          rblk * 512 + (h + 1) * 128,
                                          (qi % 2) * 512:(qi % 2) * 512 + 512],
                                in_=aosl[sl2]), s_stg[sl2], 16)
                ao_stg_mark[sl2] = (s_stg[sl2], stv)
                if h < 3 and qi >= 8:
                    for _ in range(12):
                        emit_quant_piece()
            kv_free[h % 2] = (s_pe, s_pe.v)
            if h + 2 < HPC:
                kvbufs[h + 2] = emit_kv_load(h + 2)
        while qp_i < len(quant_pieces):
            emit_quant_piece()
        for wi, (loc, glob, cci) in enumerate([(wg_qloc, wg_q, 1), (wu_qloc, wu_q, 2),
                                               (wd_qloc, wd_q, 3)]):
            for m_ in quant_weight_done[wi]:
                em.waitm("gpsimd", m_)
            em.op("gpsimd", lambda e, loc=loc, glob=glob: e.collective_compute(
                "AllGather", AluOpType.bypass, replica_groups=[list(range(NCORES))],
                ins=[loc[:, :]], outs=[glob[:, :]]), s_cc[cci], 1)
        ao_done = [(s_stg[i], s_stg[i].v) for i in range(2)]
        p3_dve_end = s_dve.v

        dump_done = None
        if debug:
            for m_ in ao_done:
                em.waitm("sync", m_)
            for b in range(32):
                dump_done = em.op("sync", lambda e, b=b:
                                  e.dma_start(
                                      out=dbg["d_aoin"][b * 128:(b + 1) * 128, :],
                                      in_=ao_in[b * 128:(b + 1) * 128, :]),
                                  s_sts, 16)

        for m_ in ao_done:
            em.waitm("gpsimd", m_)
        if debug and dump_done is not None:
            em.wait("gpsimd", s_sts, dump_done)
        em.op("gpsimd", lambda e: e.collective_compute(
            "AllToAll", AluOpType.bypass, replica_groups=[list(range(NCORES))],
            ins=[ao_in[:, :]], outs=[ao_t[:, :]]), s_cc[4], 1)

        if debug:
            em.wait("sync", s_cc[4], 1)
            for ic in range(32):
                em.op("sync", lambda e, ic=ic:
                      e.dma_start(out=dbg["d_aot"][ic * 128:(ic + 1) * 128, :],
                                  in_=ao_t[ic * 128:(ic + 1) * 128, :]), s_sts, 16)

        em.wait("sync", s_cc[4], 1)
        em.waitm("sync", kv_free[0])
        em.waitm("sync", kv_free[1])
        for ic in range(32):
            em.op("sync", lambda e, ic=ic:
                  e.dma_start(out=A1[:, ic * 1024:(ic + 1) * 1024],
                              in_=ao_t[ic * 128:(ic + 1) * 128, :]), s_ldr, 16)
        attT_lds = s_ldr.v

        wo_slice_mark = [None, None]
        zstg = [F1[:, 0:512], F1[:, 512:1024]]
        zstg_mark = [None, None]
        bo_row_b = A1[0:1, 32768:36864]
        ones_row_b2 = A1[0:1, 36864:36992]
        bol = em.op("gpsimd", lambda e: e.dma_start(out=bo_row_b,
                                                    in_=bo_row[:, :]), s_ldg, 16)
        em.op("vector", lambda e: e.memset(ones_row_b2, 1.0), s_dve, 1)
        ones2_dve = s_dve.v
        xsl = [F1[:, 5120:5632], F1[:, 5632:6144]]
        xsl_mark = [None, None]
        xsl_ld = [None, None]
        for of in range(8):
            buf = A2 if of % 2 == 0 else A3
            if of < 2:
                em.wait("gpsimd", s_dve, p3_dve_end)
                for m_ in ao_done:
                    em.waitm("gpsimd", m_)
                em.waitm("gpsimd", kv_free[1])
            else:
                em.waitm("gpsimd", wo_slice_mark[of % 2])
            for ic in range(32):
                em.op("gpsimd", lambda e, buf=buf, ic=ic, of=of:
                      e.dma_start(out=buf[:, ic * 512:(ic + 1) * 512],
                                  in_=woT[of * D + ic * 128:of * D + (ic + 1) * 128,
                                          :]), s_ldx[of % 2], 16)
            em.wait("tensor", s_ldx[of % 2], s_ldx[of % 2].v)
            if of == 0:
                em.wait("tensor", s_ldr, attT_lds)
            for t in range(8):
                bank = t % 2
                xs2 = t % 2
                em.waitm("gpsimd", xsl_mark[xs2])
                xl = em.op("gpsimd", lambda e, t=t, of=of, xs2=xs2:
                           e.dma_start(out=xsl[xs2],
                                       in_=x_own[t * 128:(t + 1) * 128,
                                                 of * 512:(of + 1) * 512]),
                           s_ldz[xs2], 16)
                xsl_ld[xs2] = (s_ldz[xs2], xl)
                emits = []
                for ic in range(32):
                    emits.append(lambda e, ic=ic, t=t, buf=buf, bank=bank,
                                 st=(ic == 0):
                                 e.matmul(pb[bank][:, :],
                                          A1[:, ic * 1024 + t * 128:
                                               ic * 1024 + (t + 1) * 128],
                                          buf[:, ic * 512:(ic + 1) * 512],
                                          start=st, stop=False))
                emits.append(lambda e, bank=bank, of=of:
                             e.matmul(pb[bank][:, :], ones_row_b2,
                                      bo_row_b[0:1, of * 512:(of + 1) * 512],
                                      start=False, stop=True))
                if of == 0 and t == 0:
                    em.wait("tensor", s_ldg, bol)
                    em.wait("tensor", s_dve, ones2_dve)
                mk = mm_group(bank, emits)
                s2 = t % 2
                em.waitm("vector", mk)
                em.waitm("vector", zstg_mark[s2])
                em.waitm("vector", xsl_ld[xs2])
                av = em.op("vector", lambda e, bank=bank, s2=s2, xs2=xs2:
                           e.tensor_tensor(zstg[s2], pb[bank][:, :],
                                           xsl[xs2], AluOpType.add), s_dve, 1)
                bank_free[bank] = (s_dve, av)
                xsl_mark[xs2] = (s_dve, av)
                em.wait("sync", s_dve, av)
                stv = em.op("sync", lambda e, t=t, of=of, s2=s2:
                            e.dma_start(out=zd[t * 128:(t + 1) * 128,
                                               of * 512:(of + 1) * 512],
                                        in_=zstg[s2]), s_stg[s2], 16)
                zstg_mark[s2] = (s_stg[s2], stv)
            wo_slice_mark[of % 2] = (s_pe, s_pe.v)
        zd_done = [(s_stg[i], s_stg[i].v) for i in range(2)]

        if debug:
            for m_ in zd_done:
                em.waitm("sync", m_)
            for t in range(8):
                em.op("sync", lambda e, t=t:
                      e.dma_start(out=dbg["d_zd"][t * 128:(t + 1) * 128, :],
                                  in_=zd[t * 128:(t + 1) * 128, :]), s_sts, 16)

        mean_ap = CF[:, 208:209]
        var_ap = CF[:, 209:210]
        rstd_ap = CF[:, 210:211]
        ln_tmp = CF[:, 211:212]

        def emit_ln(r_reg, alt_reg, out32_reg, outbf_reg):
            v_ = em.op("vector", lambda e: e.tensor_reduce(ln_tmp, r_reg,
                                                           axis=AX,
                                                           op=AluOpType.add),
                       s_dve, 1)
            em.wait("vector", s_dve, v_)
            v_ = em.op("vector", lambda e: e.tensor_scalar(mean_ap, ln_tmp,
                                                           1.0 / D, None,
                                                           AluOpType.mult,
                                                           AluOpType.bypass),
                       s_dve, 1)
            em.wait("vector", s_dve, v_)
            v_ = em.op("vector", lambda e: e.tensor_scalar(alt_reg, r_reg,
                                                           mean_ap, None,
                                                           AluOpType.subtract,
                                                           AluOpType.bypass),
                       s_dve, 1)
            em.wait("vector", s_dve, v_)
            v_ = em.op("vector", lambda e: e.tensor_tensor(r_reg, alt_reg,
                                                           alt_reg,
                                                           AluOpType.mult),
                       s_dve, 1)
            em.wait("vector", s_dve, v_)
            v_ = em.op("vector", lambda e: e.tensor_reduce(ln_tmp, r_reg,
                                                           axis=AX,
                                                           op=AluOpType.add),
                       s_dve, 1)
            em.wait("vector", s_dve, v_)
            dv = em.op("vector", lambda e: e.tensor_scalar(var_ap, ln_tmp,
                                                           1.0 / D, EPS,
                                                           AluOpType.mult,
                                                           AluOpType.add),
                       s_dve, 1)
            em.wait("scalar", s_dve, dv)
            av_ = em.op("scalar", lambda e: e.activation(ln_tmp, var_ap, AF.Ln),
                        s_act, 1)
            em.wait("scalar", s_act, av_)
            av = em.op("scalar", lambda e: e.activation(rstd_ap, ln_tmp, AF.Exp,
                                                        scale=-0.5), s_act, 1)
            em.wait("vector", s_act, av)
            marks = []
            if outbf_reg is not None:
                mv = em.op("vector", lambda e:
                           e.tensor_scalar(outbf_reg, alt_reg, rstd_ap, None,
                                           AluOpType.mult, AluOpType.bypass),
                           s_dve, 1)
                marks.append(mv)
            if out32_reg is not None:
                mv = em.op("vector", lambda e:
                           e.tensor_scalar(out32_reg, alt_reg, rstd_ap, None,
                                           AluOpType.mult, AluOpType.bypass),
                           s_dve, 1)
                marks.append(mv)
            return (s_dve, marks[-1])

        xn_stg_mark = None
        for t in range(8):
            if t == 0:
                for m_ in zd_done:
                    em.waitm("sync", m_)
            if xn_stg_mark is not None:
                em.waitm("sync", xn_stg_mark)
            lv1 = em.op("sync", lambda e, t=t:
                        e.dma_start(out=F1[:, 0:4096],
                                    in_=zd[t * 128:(t + 1) * 128, :]), s_ldr, 16)
            em.wait("vector", s_ldr, lv1)
            if xn_stg_mark is not None:
                em.waitm("vector", xn_stg_mark)
            mk = emit_ln(F1[:, 0:4096], F1[:, 4096:8192],
                         F1[:, 0:4096], A2[:, 0:4096])
            em.waitm("sync", mk)
            st1 = em.op("sync", lambda e, t=t:
                        e.dma_start(out=xnbfd[t * 128:(t + 1) * 128, :],
                                    in_=A2[:, 0:4096]), s_stg[2], 16)
            st2 = em.op("sync", lambda e, t=t:
                        e.dma_start(out=xn32d[t * 128:(t + 1) * 128, :],
                                    in_=F1[:, 0:4096]), s_stg[2], 16)
            xn_stg_mark = (s_stg[2], st2)
        xn_done = (s_stg[2], s_stg[2].v)

        if debug:
            em.waitm("sync", xn_done)
            for t in range(8):
                lv = em.op("sync", lambda e, t=t:
                           e.dma_start(out=F1[:, 0:4096],
                                       in_=xn32d[t * 128:(t + 1) * 128, :]),
                           s_ldr, 16)
                em.wait("sync", s_ldr, lv)
                em.op("sync", lambda e, t=t:
                      e.dma_start(out=dbg["d_xn"][t * 128:(t + 1) * 128, :],
                                  in_=F1[:, 0:4096]), s_sts, 16)

        interT = A1[:, 0:22528]
        wd_slots = [A1[:, 22528:33792], A1[:, 33792:45056]]
        wg_slots = [A3[:, 0:8192], A3[:, 8192:16384]]

        em.wait("sync", s_cc[1], 1)
        em.wait("sync", s_cc[2], 1)
        em.wait("sync", s_cc[3], 1)

        xnT_mark = None
        inter_free = None
        gw_slot_mark = [None, None]
        wd_slot_mark = [None, None]
        dstg = [F1[:, 0:256], F1[:, 256:512]]
        dstg_mark = [None, None]
        z2a_slots = [F1[:, 1024:1280], F1[:, 1280:1536]]
        xns_slots = [F1[:, 1536:1792], F1[:, 1792:2048]]
        r2s = [F1[:, 2048:2304], F1[:, 2304:2560]]
        r2s_mark = [None, None]
        ih1_slot_mark = [None, None]
        gtmp = F1[:, 6144:6656]

        y_final = None
        for th in range(2):
            em.waitm("sync", xn_done)
            em.waitm("sync", xnT_mark)
            for fc in range(NFC):
                em.op("sync", lambda e, fc=fc, th=th:
                      e.dma_start(out=A2[:, fc * 512:(fc + 1) * 512],
                                  in_=xnbfd[th * 512:(th + 1) * 512,
                                            fc * 128:(fc + 1) * 128],
                                  transpose=True), s_ldm, 16)
            xnT_lds = s_ldm.v
            for ih in range(2):
                inter_dve = None
                for z in range(44):
                    ic = 44 * ih + z
                    sl = z % 2
                    em.waitm("sync", gw_slot_mark[sl])
                    em.op("sync", lambda e, ic=ic, sl=sl:
                          e.dma_start(out=wg_slots[sl][:, 0:4096],
                                      in_=wg_q[ic * 128:(ic + 1) * 128, :]),
                          s_ldw[sl], 16)
                    lv = em.op("sync", lambda e, ic=ic, sl=sl:
                               e.dma_start(out=wg_slots[sl][:, 4096:8192],
                                           in_=wu_q[ic * 128:(ic + 1) * 128, :]),
                               s_ldw[sl], 16)
                    pgb, pub = 2 * sl, 1 + 2 * sl
                    em.wait("tensor", s_ldw[sl], lv)
                    if z < 2:
                        em.wait("tensor", s_ldm, xnT_lds)
                    if z == 0:
                        em.waitm("tensor", inter_free)
                    emg = []
                    emu = []
                    for fc in range(NFC):
                        emg.append(lambda e, fc=fc, sl=sl, pgb=pgb, st=(fc == 0),
                                   sp=(fc == NFC - 1):
                                   e.matmul(pb[pgb][:, :],
                                            wg_slots[sl][:, fc * 128:(fc + 1) * 128],
                                            A2[:, fc * 512:(fc + 1) * 512],
                                            start=st, stop=sp))
                        emu.append(lambda e, fc=fc, sl=sl, pub=pub, st=(fc == 0),
                                   sp=(fc == NFC - 1):
                                   e.matmul(pb[pub][:, :],
                                            wg_slots[sl][:, 4096 + fc * 128:
                                                         4096 + (fc + 1) * 128],
                                            A2[:, fc * 512:(fc + 1) * 512],
                                            start=st, stop=sp))
                    mm_group(pgb, emg)
                    mk = mm_group(pub, emu)
                    gw_slot_mark[sl] = (s_pe, mk[1])
                    em.waitm("scalar", mk)
                    av = em.op("scalar", lambda e, pgb=pgb:
                               e.copy(gtmp, pb[pgb][:, :]), s_act, 1)
                    bank_free[pgb] = (s_act, av)
                    em.wait("vector", s_act, av)
                    dv = em.op("vector", lambda e, z=z, pub=pub:
                               e.tensor_tensor(interT[:, z * 512:(z + 1) * 512],
                                               pb[pub][:, :], gtmp,
                                               AluOpType.mult), s_dve, 1)
                    bank_free[pub] = (s_dve, dv)
                    inter_dve = dv
                em.wait("tensor", s_dve, inter_dve)
                if ih == 1:
                    em.waitm("sync", dstg_mark[0])
                    em.waitm("sync", dstg_mark[1])
                rr0 = 4 * ih
                for b in range(16):
                    sl = b % 2
                    em.waitm("sync", wd_slot_mark[sl])
                    lv = None
                    for ri in range(4):
                        r = rr0 + ri
                        lv = em.op("sync", lambda e, r=r, b=b, sl=sl, ri=ri:
                                   e.dma_start(
                                       out=wd_slots[sl][:, ri * 2816:
                                                        (ri + 1) * 2816],
                                       in_=wd_q[r * 2048 + b * 128:
                                                r * 2048 + (b + 1) * 128, :]),
                                   s_ldd[sl], 16)
                    em.wait("tensor", s_ldd[sl], lv)
                    for t in range(4):
                        bank = 4 + (b * 4 + t) % 2
                        emits = []
                        for z in range(44):
                            ri, icl = z // 11, z % 11
                            emits.append(
                                lambda e, z=z, t=t, ri=ri, icl=icl, sl=sl,
                                bank=bank, st=(z == 0), sp=(z == 43):
                                e.matmul(pb[bank][:, 0:256],
                                         interT[:, z * 512 + t * 128:
                                                z * 512 + (t + 1) * 128],
                                         wd_slots[sl][:, ri * 2816 + icl * 256:
                                                      ri * 2816 +
                                                      (icl + 1) * 256],
                                         start=st, stop=sp))
                        mk = mm_group(bank, emits)
                        trow = th * 512 + t * 128
                        if ih == 0:
                            s2 = (b * 4 + t) % 2
                            em.waitm("vector", dstg_mark[s2])
                            em.waitm("vector", mk)
                            dv = em.op("vector", lambda e, bank=bank, s2=s2:
                                       e.tensor_scalar(dstg[s2], pb[bank][:, 0:256],
                                                       stot_ap, None,
                                                       AluOpType.mult,
                                                       AluOpType.bypass),
                                       s_dve, 1)
                            bank_free[bank] = (s_dve, dv)
                            em.wait("sync", s_dve, dv)
                            stv = em.op("sync", lambda e, trow=trow, b=b, s2=s2:
                                        e.dma_start(
                                            out=z2p[trow:trow + 128,
                                                    b * 256:(b + 1) * 256],
                                            in_=dstg[s2]), s_stg[s2], 16)
                            dstg_mark[s2] = (s_stg[s2], stv)
                        else:
                            s2 = (b * 4 + t) % 2
                            em.waitm("sync", ih1_slot_mark[s2])
                            lza = em.op("sync", lambda e, trow=trow, b=b, s2=s2:
                                        e.dma_start(out=z2a_slots[s2],
                                                    in_=z2p[trow:trow + 128,
                                                            b * 256:
                                                            (b + 1) * 256]),
                                        s_ldz[s2], 16)
                            lxn = em.op("sync", lambda e, trow=trow, b=b, s2=s2:
                                        e.dma_start(out=xns_slots[s2],
                                                    in_=xn32d[trow:trow + 128,
                                                              b * 256:
                                                              (b + 1) * 256]),
                                        s_ldz[s2], 16)
                            em.wait("vector", s_ldz[s2], lxn)
                            em.waitm("vector", mk)
                            em.waitm("vector", r2s_mark[s2])
                            v_ = em.op("vector", lambda e, bank=bank, s2=s2:
                                       e.scalar_tensor_tensor(dstg[s2],
                                                              pb[bank][:, 0:256],
                                                              stot_ap,
                                                              z2a_slots[s2],
                                                              AluOpType.mult,
                                                              AluOpType.add),
                                       s_dve, 1)
                            em.wait("vector", s_dve, v_)
                            dv = em.op("vector", lambda e, s2=s2:
                                       e.tensor_add(r2s[s2], dstg[s2],
                                                    xns_slots[s2]), s_dve, 1)
                            bank_free[bank] = (s_dve, dv)
                            ih1_slot_mark[s2] = (s_dve, dv)
                            em.wait("sync", s_dve, dv)
                            stv = em.op("sync", lambda e, trow=trow, b=b, s2=s2:
                                        e.dma_start(
                                            out=r2d[trow:trow + 128,
                                                    b * 256:(b + 1) * 256],
                                            in_=r2s[s2]), s_stg[s2], 16)
                            r2s_mark[s2] = (s_stg[s2], stv)
                    wd_slot_mark[sl] = (s_pe, s_pe.v)
                inter_free = (s_pe, s_pe.v)
            r2_done = [(s_stg[i], s_stg[i].v) for i in range(2)]
            for t in range(4):
                trow = th * 512 + t * 128
                if t == 0:
                    for m_ in r2_done:
                        em.waitm("sync", m_)
                if y_final is not None:
                    em.wait("sync", s_sts, y_final)
                lv = em.op("sync", lambda e, trow=trow:
                           e.dma_start(out=F1[:, 4096:8192],
                                       in_=r2d[trow:trow + 128, :]), s_ldr, 16)
                em.wait("vector", s_ldr, lv)
                mk = emit_ln(F1[:, 4096:8192], F1[:, 0:4096], F1[:, 4096:8192],
                             None)
                em.waitm("sync", mk)
                y_final = em.op("sync", lambda e, trow=trow:
                                e.dma_start(out=y[trow:trow + 128, :],
                                            in_=F1[:, 4096:8192]), s_sts, 16)
            xnT_mark = (s_pe, s_pe.v)

        if debug:
            em.op("sync", lambda e: e.dma_start(out=dbg["d_can"][:, 512:1024],
                                                in_=ao_in[2048:2560, 0:512]),
                  s_sts, 16)
            y_final = s_sts.v
        em.wait("sync", s_sts, y_final)

        with allow_ici_all_to_all(), nc.Block() as block:
            @block.sync
            def _(e):
                for fn in em.tape["sync"]:
                    fn(e)

            @block.scalar
            def _(e):
                for fn in em.tape["scalar"]:
                    fn(e)

            @block.vector
            def _(e):
                for fn in em.tape["vector"]:
                    fn(e)

            @block.tensor
            def _(e):
                for fn in em.tape["tensor"]:
                    fn(e)

            @block.gpsimd
            def _(e):
                for fn in em.tape["gpsimd"]:
                    fn(e)

    return nc


_NC_CACHE = {}


def _get_nc(debug=False):
    if debug not in _NC_CACHE:
        _NC_CACHE[debug] = build_program(debug=debug)
    return _NC_CACHE[debug]


def make_in_maps(L0_attn_x, L0_attn_Wq, L0_attn_bq, L0_attn_Wk, L0_attn_bk,
                 L0_attn_Wv, L0_attn_bv, L0_attn_Wo, L0_attn_bo,
                 L0_ffn_w_gate, L0_ffn_w_up, L0_ffn_w_down):
    f = np.float32
    x = np.asarray(L0_attn_x, f).reshape(L, D)
    xT = np.ascontiguousarray(
        x.T.reshape(D, NWIN, 512).transpose(1, 0, 2).reshape(NWIN * D, 512))
    Wq = np.asarray(L0_attn_Wq, f)
    Wk = np.asarray(L0_attn_Wk, f)
    Wv = np.asarray(L0_attn_Wv, f)
    Wo = np.asarray(L0_attn_Wo, f)
    bq = np.asarray(L0_attn_bq, f)
    bk = np.asarray(L0_attn_bk, f)
    bv = np.asarray(L0_attn_bv, f)
    bo = np.asarray(L0_attn_bo, f)
    wg = np.asarray(L0_ffn_w_gate, f)
    wu = np.asarray(L0_ffn_w_up, f)
    wd = np.asarray(L0_ffn_w_down, f)

    woT = np.ascontiguousarray(
        Wo.T.reshape(D, 8, 512).transpose(1, 0, 2).reshape(8 * D, 512))
    bo_row = np.ascontiguousarray(bo.reshape(1, D))

    wgp = np.zeros((IPAD, D), f)
    wgp[:I_TRUE] = wg
    wup = np.zeros((IPAD, D), f)
    wup[:I_TRUE] = wu
    wdTp = np.zeros((IPAD, D), f)
    wdTp[:I_TRUE] = wd.T

    mk = np.zeros((4, 128, 512), np.float32)
    kk = np.arange(128)[:, None]
    qq = np.arange(512)[None, :]
    for dj in range(4):
        mk[dj] = (128 * dj + kk <= qq).astype(np.float32)
    masks = mk.reshape(512, 512).astype(ml_dtypes.bfloat16)

    in_maps = []
    for c in range(NCORES):
        hs = c * HPC * HD
        wqT_c = np.ascontiguousarray(Wq[hs:hs + 512, :].T)
        wkT_c = np.ascontiguousarray(Wk[hs:hs + 512, :].T)
        wvT_c = np.ascontiguousarray(Wv[hs:hs + 512, :].T)
        bqk_c = np.zeros((128, 8), f)
        for h in range(HPC):
            bqk_c[:, h] = bq[hs + h * 128: hs + (h + 1) * 128]
            bqk_c[:, 4 + h] = bk[hs + h * 128: hs + (h + 1) * 128]
        bv_c = np.ascontiguousarray(bv[hs:hs + 512].reshape(1, 512))
        sh = wgp[c * ISH:(c + 1) * ISH, :]
        wgT_c = np.ascontiguousarray(
            sh.T.reshape(NFC, 128, NICL, 128).transpose(2, 1, 0, 3)
            .reshape(ISH, D))
        sh = wup[c * ISH:(c + 1) * ISH, :]
        wuT_c = np.ascontiguousarray(
            sh.T.reshape(NFC, 128, NICL, 128).transpose(2, 1, 0, 3)
            .reshape(ISH, D))
        sh = wdTp[c * ISH:(c + 1) * ISH, :]
        wdT_c = np.ascontiguousarray(
            sh.reshape(NICL, 128, 16, 256).transpose(2, 1, 0, 3)
            .reshape(2048, 2816))
        in_maps.append({
            "xT": xT,
            "x_own": np.ascontiguousarray(x[c * TPC:(c + 1) * TPC, :]),
            "wqT": wqT_c, "wkT": wkT_c, "wvT": wvT_c,
            "bqk": bqk_c, "bv_row": bv_c,
            "woT": woT, "bo_row": bo_row,
            "wgT_sh": wgT_c, "wuT_sh": wuT_c, "wdT_sh": wdT_c,
            "masks": masks,
        })
    return in_maps


def kernel(**inputs):
    in_maps = make_in_maps(**inputs)
    nc = _get_nc(debug=False)
    res = run_bass_kernel_spmd(nc, in_maps, core_ids=list(range(NCORES)))
    out = np.concatenate([np.asarray(res.results[c]["y"]) for c in range(NCORES)],
                         axis=0)
    return out.reshape(1, L, D).astype(np.float32)


# version 15 (final)
# speedup vs baseline: 1.0374x; 1.0374x over previous
import contextlib
import numpy as np
import ml_dtypes
import concourse.bass as bass
import concourse.mybir as mybir
from concourse.bass_utils import run_bass_kernel_spmd
from concourse.alu_op_type import AluOpType
from concourse.aot_env import allow_ici_all_to_all

F32 = mybir.dt.float32
BF16 = mybir.dt.bfloat16
AF = mybir.ActivationFunctionType
AX = mybir.AxisListType.X

NCORES = 8
D = 4096
L = 8192
HD = 128
HPC = 4
TPC = 1024
NFC = 32
I_TRUE = 11008
ISH = 1408
IPAD = 11264
NICL = 11
WTOK = 512
NWIN = L // WTOK
QC = 512
NQC = L // QC
EPS = 1e-5
ISQHD = 0.08838834764831845
WMEAN_N = float(I_TRUE * D)

ENGS = ("sync", "scalar", "vector", "tensor", "gpsimd")


class Sem:
    def __init__(self, h):
        self.h = h
        self.v = 0

    def bump(self, n):
        self.v += n
        return self.v


class Em:
    def __init__(self):
        self.tape = {k: [] for k in ENGS}

    def op(self, eng, fn, sem=None, inc=1):
        if sem is None:
            self.tape[eng].append(fn)
            return None
        v = sem.bump(inc)
        self.tape[eng].append(lambda e, fn=fn, h=sem.h, i=inc: fn(e).then_inc(h, i))
        return v

    def wait(self, eng, sem, v):
        if v is None or v <= 0:
            return
        self.tape[eng].append(lambda e, h=sem.h, vv=v: e.wait_ge(h, vv))

    def waitm(self, eng, mark):
        if mark is not None:
            self.wait(eng, mark[0], mark[1])


def build_program(debug=False):
    nc = bass.Bass("TRN2", target_bir_lowering=False)

    xT = nc.dram_tensor("xT", [NWIN * D, 512], F32, kind="ExternalInput").ap()
    x_own = nc.dram_tensor("x_own", [TPC, D], F32, kind="ExternalInput").ap()
    wqT = nc.dram_tensor("wqT", [D, HPC * HD], F32, kind="ExternalInput").ap()
    wkT = nc.dram_tensor("wkT", [D, HPC * HD], F32, kind="ExternalInput").ap()
    wvT = nc.dram_tensor("wvT", [D, HPC * HD], F32, kind="ExternalInput").ap()
    bqk = nc.dram_tensor("bqk", [128, 8], F32, kind="ExternalInput").ap()
    bv_row = nc.dram_tensor("bv_row", [1, 512], F32, kind="ExternalInput").ap()
    woT = nc.dram_tensor("woT", [8 * D, 512], F32, kind="ExternalInput").ap()
    bo_row = nc.dram_tensor("bo_row", [1, D], F32, kind="ExternalInput").ap()
    wgT_sh = nc.dram_tensor("wgT_sh", [ISH, D], F32, kind="ExternalInput").ap()
    wuT_sh = nc.dram_tensor("wuT_sh", [ISH, D], F32, kind="ExternalInput").ap()
    wdT_sh = nc.dram_tensor("wdT_sh", [2048, 2816], F32, kind="ExternalInput").ap()
    masks = nc.dram_tensor("masks", [512, 512], BF16, kind="ExternalInput").ap()

    y = nc.dram_tensor("y", [TPC, D], F32, kind="ExternalOutput").ap()

    qT_sc = nc.dram_tensor("qT_sc", [HPC * HD, L], BF16).ap()
    kT_sc = nc.dram_tensor("kT_sc", [HPC * HD, L], BF16).ap()
    v_sc = nc.dram_tensor("v_sc", [HPC * 64 * 128, 128], BF16).ap()
    ao_in = nc.dram_tensor("ao_in", [NCORES * HPC * HD, TPC], BF16).ap()
    ao_t = nc.dram_tensor("ao_t", [D, TPC], BF16).ap()
    wg_qloc = nc.dram_tensor("wg_qloc", [ISH, D], BF16).ap()
    wu_qloc = nc.dram_tensor("wu_qloc", [ISH, D], BF16).ap()
    wd_qloc = nc.dram_tensor("wd_qloc", [2048, 2816], BF16).ap()
    wg_q = nc.dram_tensor("wg_q", [IPAD, D], BF16, addr_space="Shared").ap()
    wu_q = nc.dram_tensor("wu_q", [IPAD, D], BF16, addr_space="Shared").ap()
    wd_q = nc.dram_tensor("wd_q", [16384, 2816], BF16, addr_space="Shared").ap()
    accd = nc.dram_tensor("accd", [3, 128], F32).ap()
    scl_in = nc.dram_tensor("scl_in", [1, 8], F32).ap()
    scl_out = nc.dram_tensor("scl_out", [1, 8], F32).ap()
    zd = nc.dram_tensor("zd", [TPC, D], F32).ap()
    xn32d = nc.dram_tensor("xn32d", [TPC, D], F32).ap()
    xnbfd = nc.dram_tensor("xnbfd", [TPC, D], BF16).ap()
    z2p = nc.dram_tensor("z2p", [TPC, D], F32).ap()
    r2d = nc.dram_tensor("r2d", [TPC, D], F32).ap()

    dbg = {}
    if debug:
        for name, shape, dt in [
            ("d_qT", [HPC * HD, L], BF16), ("d_kT", [HPC * HD, L], BF16),
            ("d_v", [HPC * 64 * 128, 128], BF16), ("d_aot", [D, TPC], BF16),
            ("d_xn", [TPC, D], F32), ("d_wg", [ISH, D], BF16),
            ("d_scl", [1, 8], F32), ("d_cneg", [128, 8], F32),
            ("d_zd", [TPC, D], F32), ("d_aoin", [NCORES * HPC * HD, TPC], BF16),
            ("d_sums", [64, 512], F32), ("d_ps", [128, 1536], F32),
            ("d_can", [512, 1024], BF16),
        ]:
            dbg[name] = nc.dram_tensor(name, shape, dt, kind="ExternalOutput").ap()

    em = Em()

    with contextlib.ExitStack() as ctx:
        A1 = ctx.enter_context(nc.sbuf_tensor("A1", [128, 49152], BF16))
        A2 = ctx.enter_context(nc.sbuf_tensor("A2", [128, 16384], BF16))
        A3 = ctx.enter_context(nc.sbuf_tensor("A3", [128, 16384], BF16))
        F1 = ctx.enter_context(nc.sbuf_tensor("F1", [128, 8192], F32))
        CB = ctx.enter_context(nc.sbuf_tensor("CB", [128, 1537], BF16))
        CF = ctx.enter_context(nc.sbuf_tensor("CF", [128, 1280], F32))
        QB = ctx.enter_context(nc.sbuf_tensor("QB", [128, 1024], BF16))
        pb = [ctx.enter_context(nc.psum_tensor(f"pb{i}", [128, 512], F32))
              for i in range(8)]

        s_lds = Sem(ctx.enter_context(nc.semaphore("lds")))
        s_ldg = Sem(ctx.enter_context(nc.semaphore("ldg")))
        s_ldv = Sem(ctx.enter_context(nc.semaphore("ldv")))
        s_sts = Sem(ctx.enter_context(nc.semaphore("sts")))
        s_stv = Sem(ctx.enter_context(nc.semaphore("stv")))
        s_pe = Sem(ctx.enter_context(nc.semaphore("pe")))
        s_act = Sem(ctx.enter_context(nc.semaphore("act")))
        s_dve = Sem(ctx.enter_context(nc.semaphore("dve")))
        s_cc = [Sem(ctx.enter_context(nc.semaphore(f"cc{i}"))) for i in range(5)]
        s_gate = Sem(ctx.enter_context(nc.semaphore("gate")))
        s_ldx = [Sem(ctx.enter_context(nc.semaphore(f"ldx{i}"))) for i in range(2)]
        s_ldq = [Sem(ctx.enter_context(nc.semaphore(f"ldq{i}"))) for i in range(2)]
        s_ldkv = [Sem(ctx.enter_context(nc.semaphore(f"ldkv{i}"))) for i in range(2)]
        s_ldqt = [Sem(ctx.enter_context(nc.semaphore(f"ldqt{i}"))) for i in range(2)]
        s_ldm = Sem(ctx.enter_context(nc.semaphore("ldm")))
        s_ldw = [Sem(ctx.enter_context(nc.semaphore(f"ldw{i}"))) for i in range(2)]
        s_ldd = [Sem(ctx.enter_context(nc.semaphore(f"ldd{i}"))) for i in range(2)]
        s_ldz = [Sem(ctx.enter_context(nc.semaphore(f"ldz{i}"))) for i in range(2)]
        s_ldr = Sem(ctx.enter_context(nc.semaphore("ldr")))
        s_stg = [Sem(ctx.enter_context(nc.semaphore(f"stg{i}"))) for i in range(3)]
        s_stq = [Sem(ctx.enter_context(nc.semaphore(f"stq{i}"))) for i in range(2)]

        ones_col = CB[:, 0:1]
        stg = [CB[:, 1:513], CB[:, 513:1025], CB[:, 1025:1537]]
        ones_row_f = CF[0:1, 1:129]
        bqk_sb = CF[:, 136:144]
        cneg_sb = CF[:, 144:148]
        t_ap = CF[:, 156:159]
        negt_ap = CF[:, 159:162]
        stot_ap = CF[:, 162:163]
        ssum_sb = CF[0:1, 256:768]
        rs_sb = CF[0:1, 768:1280]
        bc_sb = F1[:, 2048:2560]
        bv_bc = F1[:, 5632:6144]

        bank_free = [None] * 8

        def mm_group(bank_i, emits):
            em.waitm("tensor", bank_free[bank_i])
            v = None
            for i, fn in enumerate(emits):
                if i == len(emits) - 1:
                    v = em.op("tensor", fn, s_pe, 1)
                else:
                    em.op("tensor", fn)
            return (s_pe, v)

        if debug:
            em.op("sync", lambda e: e.dma_start(out=ao_in[2048:2560, 0:512],
                                                in_=masks[:, :]), s_sts, 16)
        em.op("sync", lambda e: e.dma_start(out=bqk_sb, in_=bqk[:, :]), s_lds, 16)
        p0_lds = s_lds.v
        bvl = em.op("sync", lambda e: e.dma_start(out=CF[0:1, 256:768],
                                                  in_=bv_row[:, :]), s_lds, 16)
        em.op("vector", lambda e: e.memset(ones_col, 1.0))
        em.op("vector", lambda e: e.memset(CF[0:1, 1:129], 1.0))
        em.op("vector", lambda e: e.memset(CF[0:1, 148:156], 0.0), s_dve, 1)
        consts_dve = s_dve.v
        em.wait("tensor", s_lds, bvl)
        em.wait("tensor", s_dve, consts_dve)
        pv = em.op("tensor", lambda e: e.matmul(pb[7][:, :], ones_row_f,
                                                CF[0:1, 256:768], start=True,
                                                stop=True), s_pe, 1)
        em.wait("scalar", s_pe, pv)
        av = em.op("scalar", lambda e: e.copy(bv_bc, pb[7][:, :]), s_act, 1)
        bank_free[7] = (s_act, av)
        bvbc_done = (s_act, av)

        em.op("vector", lambda e: e.memset(CF[:, 384:480], 0.0))
        wsrcs = [(wgT_sh, ISH, D), (wuT_sh, ISH, D), (wdT_sh, 2048, 2816)]
        p1_slots = [F1[:, 0:2048], F1[:, 2048:4096]]
        p1_pieces = []
        for wi, (src, rows, cols) in enumerate(wsrcs):
            widths = [2048] * (cols // 2048) + ([cols % 2048] if cols % 2048 else [])
            for ti in range(rows // 128):
                off = 0
                for wdt in widths:
                    p1_pieces.append((wi, src, ti, off, wdt))
                    off += wdt
        p1_i = 0
        p1_slot_mark = [None, None]

        def emit_p1_piece():
            nonlocal p1_i
            if p1_i >= len(p1_pieces):
                return
            wi, src, ti, off, wdt = p1_pieces[p1_i]
            sl = p1_i % 2
            em.waitm("sync", p1_slot_mark[sl])
            lv = em.op("sync",
                       lambda e, sl=sl, src=src, ti=ti, off=off, wdt=wdt:
                       e.dma_start(out=p1_slots[sl][:, 0:wdt],
                                   in_=src[ti * 128:(ti + 1) * 128,
                                           off:off + wdt]), s_ldq[sl], 16)
            em.wait("vector", s_ldq[sl], lv)
            col = 384 + p1_i
            dv = em.op("vector", lambda e, sl=sl, wdt=wdt, col=col:
                       e.tensor_reduce(CF[:, col:col + 1],
                                       p1_slots[sl][:, 0:wdt],
                                       axis=AX, op=AluOpType.add,
                                       apply_absolute_value=True), s_dve, 1)
            p1_slot_mark[sl] = (s_dve, dv)
            p1_i += 1

        def emit_p1_finalize():
            bounds = []
            n0 = 0
            for wsrc_, rows_, cols_ in wsrcs:
                n_p = (rows_ // 128) * ((cols_ + 2047) // 2048)
                bounds.append((n0, n0 + n_p))
                n0 += n_p
            em.wait("vector", s_dve, s_dve.v)
            dvv = None
            for wi in range(3):
                b0, b1 = bounds[wi]
                dvv = em.op("vector", lambda e, wi=wi, b0=b0, b1=b1:
                            e.tensor_reduce(CF[:, 163 + wi:164 + wi],
                                            CF[:, 384 + b0:384 + b1],
                                            axis=AX, op=AluOpType.add),
                            s_dve, 1)
            em.wait("gpsimd", s_dve, dvv)
            for wi in range(3):
                em.op("gpsimd", lambda e, wi=wi:
                      e.dma_start(out=accd[wi:wi + 1, :],
                                  in_=CF[:, 163 + wi:164 + wi]), s_stv, 16)
            em.wait("gpsimd", s_stv, s_stv.v)
            lv = em.op("gpsimd", lambda e:
                       e.dma_start(out=CF[0:1, 896:1280],
                                   in_=accd[:, :]), s_ldg, 16)
            em.wait("gpsimd", s_ldg, lv)
            gv = em.op("gpsimd", lambda e: e.nop(), s_gate, 1)
            em.wait("vector", s_gate, gv)
            for wi in range(3):
                em.op("vector", lambda e, wi=wi:
                      e.tensor_reduce(CF[0:1, 168 + wi:169 + wi],
                                      CF[0:1, 896 + wi * 128:1024 + wi * 128],
                                      axis=AX, op=AluOpType.add))
            em.op("vector", lambda e: e.memset(CF[0:1, 171:176], 0.0), s_dve, 1)
            sder = s_dve.v
            em.wait("gpsimd", s_dve, sder)
            sv = em.op("gpsimd", lambda e: e.dma_start(out=scl_in[:, :],
                                                       in_=CF[0:1, 168:176]),
                       s_stv, 16)
            em.wait("gpsimd", s_stv, sv)
            em.op("gpsimd", lambda e: e.collective_compute(
                "AllReduce", AluOpType.add, replica_groups=[list(range(NCORES))],
                ins=[scl_in[:, :]], outs=[scl_out[:, :]]), s_cc[0], 1)
            em.wait("gpsimd", s_cc[0], 1)
            lv = em.op("gpsimd", lambda e: e.dma_start(out=CF[0:1, 176:184],
                                                       in_=scl_out[:, :]),
                       s_ldg, 16)
            em.wait("gpsimd", s_ldg, lv)
            gv = em.op("gpsimd", lambda e: e.nop(), s_gate, 1)
            em.wait("vector", s_gate, gv)
            v_ = em.op("vector", lambda e: e.tensor_scalar(CF[0:1, 184:187],
                                                           CF[0:1, 176:179],
                                                           1.0 / WMEAN_N, None,
                                                           AluOpType.mult,
                                                           AluOpType.bypass),
                       s_dve, 1)
            em.wait("vector", s_dve, v_)
            em.op("vector", lambda e: e.tensor_scalar(CF[0:1, 187:190],
                                                      CF[0:1, 184:187],
                                                      EPS, 0.5, AluOpType.add,
                                                      AluOpType.mult))
            v_ = em.op("vector", lambda e: e.tensor_tensor(CF[0:1, 191:192],
                                                           CF[0:1, 184:185],
                                                           CF[0:1, 185:186],
                                                           AluOpType.mult),
                       s_dve, 1)
            em.wait("vector", s_dve, v_)
            v_ = em.op("vector", lambda e: e.tensor_tensor(CF[0:1, 190:191],
                                                           CF[0:1, 191:192],
                                                           CF[0:1, 186:187],
                                                           AluOpType.mult),
                       s_dve, 1)
            em.wait("vector", s_dve, v_)
            dv = em.op("vector", lambda e:
                       e.tensor_scalar(CF[0:1, 192:193], CF[0:1, 190:191], -1.0,
                                       None, AluOpType.mult, AluOpType.bypass),
                       s_dve, 1)
            return (s_dve, dv)

        def emit_scale_broadcasts(scal_mark):
            em.waitm("tensor", scal_mark)
            for k in range(4):
                src_col = 187 + k if k < 3 else 192
                em.waitm("tensor", bank_free[7])
                pv = em.op("tensor", lambda e, src_col=src_col:
                           e.matmul(pb[7][:, 0:1], ones_row_f,
                                    CF[0:1, src_col:src_col + 1], start=True,
                                    stop=True), s_pe, 1)
                em.wait("scalar", s_pe, pv)
                if k < 3:
                    av = em.op("scalar", lambda e, k=k:
                               e.copy(t_ap[:, k:k + 1], pb[7][:, 0:1]),
                               s_act, 1)
                else:
                    av = em.op("scalar", lambda e: e.copy(stot_ap,
                                                          pb[7][:, 0:1]),
                               s_act, 1)
                bank_free[7] = (s_act, av)
            em.wait("vector", s_act, bank_free[7][1])
            dv = em.op("vector", lambda e:
                       e.tensor_scalar(negt_ap, t_ap, -1.0, None,
                                       AluOpType.mult, AluOpType.bypass),
                       s_dve, 1)
            return (s_dve, dv)

        A1q, A1k, A1v = A1[:, 0:16384], A1[:, 16384:32768], A1[:, 32768:49152]
        for wsrc, adst in ((wqT, A1q), (wkT, A1k), (wvT, A1v)):
            for fc in range(NFC):
                em.op("gpsimd", lambda e, wsrc=wsrc, adst=adst, fc=fc:
                      e.dma_start(out=adst[:, fc * 512:(fc + 1) * 512],
                                  in_=wsrc[fc * 128:(fc + 1) * 128, :]), s_ldg, 16)
        wload_mark = s_ldg.v

        quant_pieces = []
        for wi, (qsrc, qdst, rows, cols) in enumerate([
                (wgT_sh, wg_qloc, ISH, D), (wuT_sh, wu_qloc, ISH, D),
                (wdT_sh, wd_qloc, 2048, 2816)]):
            for ti in range(rows // 128):
                off = 0
                widths = [512] * (cols // 512) + ([cols % 512] if cols % 512 else [])
                for wdt in widths:
                    quant_pieces.append((wi, qsrc, qdst, ti, off, wdt))
                    off += wdt
        qp_i = 0
        quant_ready = [None]
        q_slotmark = [None, None]
        qb_slotmark = [None, None]
        q_wslots = [F1[:, 4096:4608], F1[:, 4608:5120]]
        q_aslot = F1[:, 5120:5632]
        quant_weight_done = [None, None, None]

        def emit_quant_piece():
            nonlocal qp_i
            if qp_i >= len(quant_pieces):
                return
            wi, qsrc, qdst, ti, off, wdt = quant_pieces[qp_i]
            sl = qp_i % 2
            if qp_i == 0:
                em.waitm("vector", quant_ready[0])
                em.waitm("sync", quant_ready[0])
            em.waitm("vector", q_slotmark[sl])
            wslot = q_wslots[sl]
            em.waitm("sync", q_slotmark[sl])
            lv = em.op("sync",
                       lambda e, qsrc=qsrc, ti=ti, off=off, wdt=wdt, wslot=wslot:
                       e.dma_start(out=wslot[:, 0:wdt],
                                   in_=qsrc[ti * 128:(ti + 1) * 128, off:off + wdt]),
                       s_ldq[sl], 16)
            em.wait("vector", s_ldq[sl], lv)
            v_ = em.op("vector", lambda e, wslot=wslot, wdt=wdt, wi=wi:
                       e.tensor_scalar(q_aslot[:, 0:wdt], wslot[:, 0:wdt],
                                       t_ap[:, wi:wi + 1], None, AluOpType.is_gt,
                                       AluOpType.bypass), s_dve, 1)
            em.wait("vector", s_dve, v_)
            em.waitm("vector", qb_slotmark[sl])
            qb = QB[:, sl * 512:(sl + 1) * 512]
            dv = em.op("vector", lambda e, wslot=wslot, wdt=wdt, wi=wi, qb=qb:
                       e.scalar_tensor_tensor(qb[:, 0:wdt], wslot[:, 0:wdt],
                                              negt_ap[:, wi:wi + 1],
                                              q_aslot[:, 0:wdt],
                                              AluOpType.is_lt, AluOpType.subtract),
                       s_dve, 1)
            q_slotmark[sl] = (s_dve, dv)
            em.wait("sync", s_dve, dv)
            stv = em.op("sync",
                        lambda e, qdst=qdst, ti=ti, off=off, wdt=wdt, qb=qb:
                        e.dma_start(out=qdst[ti * 128:(ti + 1) * 128, off:off + wdt],
                                    in_=qb[:, 0:wdt]), s_stq[sl], 16)
            qb_slotmark[sl] = (s_stq[sl], stv)
            qp_i += 1
            if qp_i == len(quant_pieces) or quant_pieces[qp_i][0] != wi:
                quant_weight_done[wi] = [(s_stq[0], s_stq[0].v),
                                         (s_stq[1], s_stq[1].v)]

        win_done = [None] * NWIN
        stg_mark = [None, None, None]

        for w in range(NWIN):
            xbuf = A2 if w % 2 == 0 else A3
            if w >= 2:
                em.waitm("gpsimd", win_done[w - 2])
            for fc in range(NFC):
                em.op("gpsimd", lambda e, xbuf=xbuf, fc=fc, w=w:
                      e.dma_start(out=xbuf[:, fc * 512:(fc + 1) * 512],
                                  in_=xT[w * D + fc * 128:w * D + (fc + 1) * 128,
                                         :]), s_ldx[w % 2], 16)
            em.wait("tensor", s_ldx[w % 2], s_ldx[w % 2].v)
            if w == 0:
                em.wait("tensor", s_ldg, wload_mark)
                em.wait("tensor", s_lds, p0_lds)
                em.wait("tensor", s_dve, consts_dve)
            g = 0
            for h in range(HPC):
                for qk in range(2):
                    bank = g % 4
                    wmat = A1q if qk == 0 else A1k
                    emits = []
                    for fc in range(NFC):
                        emits.append(
                            lambda e, wmat=wmat, fc=fc, h=h, xbuf=xbuf, bank=bank,
                            st=(fc == 0), sp=(fc == NFC - 1):
                            e.matmul(pb[bank][:, :],
                                     wmat[:, fc * 512 + h * 128:
                                          fc * 512 + (h + 1) * 128],
                                     xbuf[:, fc * 512:(fc + 1) * 512],
                                     start=st, stop=sp))
                    mk = mm_group(bank, emits)
                    s2 = g % 2
                    em.waitm("scalar", mk)
                    em.waitm("scalar", stg_mark[s2])
                    av = em.op("scalar", lambda e, bank=bank, s2=s2, h=h, qk=qk:
                               e.activation(stg[s2], pb[bank][:, :], AF.Identity,
                                            bias=bqk_sb[:, 4 * qk + h:
                                                        4 * qk + h + 1]), s_act, 1)
                    bank_free[bank] = (s_act, av)
                    if w < 3:
                        em.waitm("scalar", stg_mark[2])
                        em.wait("scalar", s_act, av)
                        av2 = em.op("scalar", lambda e, s2=s2:
                                    e.activation(stg[2], stg[s2], AF.Square),
                                    s_act, 1)
                        nb = 6 + g % 2
                        em.waitm("tensor", bank_free[nb])
                        em.wait("tensor", s_act, av2)
                        pv = em.op("tensor", lambda e, nb=nb:
                                   e.matmul(pb[nb][0:1, :], ones_col, stg[2],
                                            start=True, stop=True), s_pe, 1)
                        stg_mark[2] = (s_pe, pv)
                        em.wait("vector", s_pe, pv)
                        col = 768 + (4 * qk + h) * 16 + w
                        dv = em.op("vector", lambda e, nb=nb, col=col:
                                   e.tensor_reduce(CF[0:1, col:col + 1],
                                                   pb[nb][0:1, :], axis=AX,
                                                   op=AluOpType.max), s_dve, 1)
                        bank_free[nb] = (s_dve, dv)
                    dst = qT_sc if qk == 0 else kT_sc
                    em.wait("sync", s_act, av)
                    stv = em.op("sync", lambda e, dst=dst, h=h, w=w, s2=s2:
                                e.dma_start(out=dst[h * 128:(h + 1) * 128,
                                                    w * 512:(w + 1) * 512],
                                            in_=stg[s2]), s_stg[s2], 16)
                    stg_mark[s2] = (s_stg[s2], stv)
                    g += 1
            for st4 in range(4):
                bank = 4 + st4 % 2
                emits = []
                for fc in range(NFC):
                    emits.append(lambda e, fc=fc, st4=st4, xbuf=xbuf, bank=bank,
                                 st=(fc == 0), sp=(fc == NFC - 1):
                                 e.matmul(pb[bank][:, :],
                                          xbuf[:, fc * 512 + st4 * 128:
                                               fc * 512 + (st4 + 1) * 128],
                                          A1v[:, fc * 512:(fc + 1) * 512],
                                          start=st, stop=sp))
                mk = mm_group(bank, emits)
                s2 = st4 % 2
                em.waitm("vector", mk)
                em.waitm("vector", stg_mark[s2])
                if w == 0 and st4 == 0:
                    em.waitm("vector", bvbc_done)
                av = em.op("vector", lambda e, bank=bank, s2=s2:
                           e.tensor_tensor(stg[s2], pb[bank][:, :], bv_bc,
                                           AluOpType.add), s_dve, 1)
                bank_free[bank] = (s_dve, av)
                em.wait("sync", s_dve, av)
                jt = w * 4 + st4
                stv = None
                for h in range(HPC):
                    stv = em.op("sync", lambda e, h=h, jt=jt, s2=s2:
                                e.dma_start(
                                    out=v_sc[(h * 64 + jt) * 128:
                                             (h * 64 + jt + 1) * 128, :],
                                    in_=stg[s2][:, h * 128:(h + 1) * 128]),
                                s_stg[s2], 16)
                stg_mark[s2] = (s_stg[s2], stv)
            win_done[w] = (s_pe, s_pe.v)
            if w < 8:
                for _ in range(10):
                    emit_p1_piece()
            elif w == 8:
                while p1_i < len(p1_pieces):
                    emit_p1_piece()
                scal_mark = emit_p1_finalize()
            elif w == 9:
                quant_ready[0] = emit_scale_broadcasts(scal_mark)
        qkv_done = [(s_stg[i], s_stg[i].v) for i in range(3)]

        em.wait("vector", s_dve, s_dve.v)
        for h in range(HPC):
            em.op("vector", lambda e, h=h:
                  e.tensor_reduce(CF[0:1, 196:197],
                                  CF[0:1, 768 + h * 16:768 + h * 16 + 3],
                                  axis=AX, op=AluOpType.max))
            v_ = em.op("vector", lambda e, h=h:
                       e.tensor_reduce(CF[0:1, 197:198],
                                       CF[0:1, 768 + (4 + h) * 16:768 + (4 + h) * 16 + 3],
                                       axis=AX, op=AluOpType.max), s_dve, 1)
            em.wait("vector", s_dve, v_)
            dv = em.op("vector", lambda e:
                       e.tensor_scalar(CF[0:1, 198:199], CF[0:1, 196:197],
                                       CF[0:1, 197:198], 1.0 / 128.0,
                                       AluOpType.mult, AluOpType.mult), s_dve, 1)
            em.wait("scalar", s_dve, dv)
            av_ = em.op("scalar", lambda e: e.activation(CF[0:1, 199:200],
                                                         CF[0:1, 198:199],
                                                         AF.Ln), s_act, 1)
            em.wait("scalar", s_act, av_)
            av = em.op("scalar", lambda e: e.activation(CF[0:1, 200:201],
                                                        CF[0:1, 199:200], AF.Exp,
                                                        scale=0.5), s_act, 1)
            em.wait("vector", s_act, av)
            dv = em.op("vector", lambda e:
                       e.tensor_scalar(CF[0:1, 201:202], CF[0:1, 200:201], 2.0, -1.0,
                                       AluOpType.add, AluOpType.mult), s_dve, 1)
            em.waitm("tensor", bank_free[7])
            em.wait("tensor", s_dve, dv)
            pv = em.op("tensor", lambda e: e.matmul(pb[7][:, 0:1], ones_row_f,
                                                    CF[0:1, 201:202], start=True,
                                                    stop=True), s_pe, 1)
            em.wait("scalar", s_pe, pv)
            av = em.op("scalar", lambda e, h=h: e.copy(cneg_sb[:, h:h + 1],
                                                       pb[7][:, 0:1]), s_act, 1)
            bank_free[7] = (s_act, av)
        cneg_done = (s_act, s_act.v)

        if debug:
            for qd in qkv_done:
                em.waitm("sync", qd)
            em.waitm("sync", cneg_done)
            for name, src in [("d_qT", qT_sc), ("d_kT", kT_sc)]:
                for h in range(4):
                    em.op("sync", lambda e, name=name, src=src, h=h:
                          e.dma_start(out=dbg[name][h * 128:(h + 1) * 128, :],
                                      in_=src[h * 128:(h + 1) * 128, :]), s_sts, 16)
            for b in range(16):
                em.op("sync", lambda e, b=b:
                      e.dma_start(out=dbg["d_v"][b * 2048:(b + 1) * 2048, :],
                                  in_=v_sc[b * 2048:(b + 1) * 2048, :]), s_sts, 16)
            em.op("sync", lambda e: e.dma_start(out=dbg["d_cneg"][:, 0:4],
                                                in_=cneg_sb), s_sts, 16)
            em.op("sync", lambda e: e.dma_start(out=dbg["d_cneg"][:, 4:7],
                                                in_=t_ap), s_sts, 16)
            em.wait("sync", s_cc[0], 1)
            lv = em.op("sync", lambda e: e.dma_start(out=CF[0:1, 640:648],
                                                     in_=scl_out[:, :]), s_lds, 16)
            em.wait("sync", s_lds, lv)
            em.op("sync", lambda e: e.dma_start(out=dbg["d_scl"][:, :],
                                                in_=CF[0:1, 640:648]), s_sts, 16)
            em.waitm("sync", (s_stq[0], s_stq[0].v))
            em.waitm("sync", (s_stq[1], s_stq[1].v))
            for ti in range(11):
                em.op("sync", lambda e, ti=ti:
                      e.dma_start(out=dbg["d_wg"][ti * 128:(ti + 1) * 128, :],
                                  in_=wg_qloc[ti * 128:(ti + 1) * 128, :]),
                      s_sts, 16)
            em.op("sync", lambda e: e.dma_start(out=dbg["d_can"][:, 0:512],
                                                in_=ao_in[2048:2560, 0:512]),
                  s_sts, 16)

        em.waitm("sync", win_done[NWIN - 2])
        for dj in range(4):
            em.op("sync", lambda e, dj=dj:
                  e.dma_start(out=A2[:, dj * 512:(dj + 1) * 512],
                              in_=masks[dj * 128:(dj + 1) * 128, :]), s_ldm, 16)
        em.wait("vector", s_ldm, s_ldm.v)
        qbuf = [A2[:, 2048:2560], A2[:, 2560:3072]]
        ptsl = [A3[:, 0:512], A3[:, 512:1024], A3[:, 1024:1536]]
        ptm = [A3[:, 2560 + k * 512:3072 + k * 512] for k in range(4)]
        aosl = [A3[:, 4608:5120], A3[:, 5120:5632]]
        kv_load_mark = [None] * HPC
        kv_free = [None, None]

        def emit_kv_load(h):
            slot = h % 2
            kbuf = A1[:, slot * 16384: slot * 16384 + 8192]
            vbuf = A1[:, slot * 16384 + 8192: slot * 16384 + 16384]
            for qd in qkv_done:
                em.waitm("sync", qd)
            em.waitm("sync", kv_free[slot])
            if h < 2:
                em.waitm("sync", win_done[NWIN - 1])
            em.op("sync", lambda e, h=h, kbuf=kbuf:
                  e.dma_start(out=kbuf, in_=kT_sc[h * 128:(h + 1) * 128, :]),
                  s_ldkv[slot], 16)
            vr = v_sc.rearrange("(j p) m -> p j m", p=128)
            lv = em.op("sync", lambda e, h=h, vbuf=vbuf, vr=vr:
                       e.dma_start(out=vbuf, in_=vr[:, h * 64:(h + 1) * 64, :]),
                       s_ldkv[slot], 16)
            kv_load_mark[h] = (s_ldkv[slot], lv)
            return kbuf, vbuf

        kvbufs = {0: emit_kv_load(0), 1: emit_kv_load(1)}
        pt_mark = [None, None, None]
        ptm_mark = [None, None, None, None]
        ao_bank_mark = [None, None]
        sums_sc_mark = None
        ao_stg_mark = [None, None]
        qload_mark = [None] * NQC
        q_refill = [None, None]

        for h in range(HPC):
            kbuf, vbuf = kvbufs[h]
            em.waitm("tensor", kv_load_mark[h])
            if h == 0:
                em.waitm("tensor", cneg_done)
            em.waitm("sync", q_refill[0])
            lv = em.op("sync", lambda e, h=h:
                       e.dma_start(out=qbuf[0],
                                   in_=qT_sc[h * 128:(h + 1) * 128, 0:512]),
                       s_ldqt[0], 16)
            qload_mark[0] = (s_ldqt[0], lv)
            for qi in range(NQC):
                nt = 4 * qi + 4
                if qi + 1 < NQC:
                    em.waitm("sync", q_refill[(qi + 1) % 2])
                    lv = em.op("sync", lambda e, h=h, qi=qi:
                               e.dma_start(out=qbuf[(qi + 1) % 2],
                                           in_=qT_sc[h * 128:(h + 1) * 128,
                                                     (qi + 1) * 512:(qi + 2) * 512]),
                               s_ldqt[(qi + 1) % 2], 16)
                    qload_mark[qi + 1] = (s_ldqt[(qi + 1) % 2], lv)
                qb = qbuf[qi % 2]
                em.waitm("tensor", qload_mark[qi])
                ao_bank = 4 + qi % 2
                em.waitm("tensor", ao_bank_mark[qi % 2])
                if sums_sc_mark is not None:
                    em.waitm("tensor", sums_sc_mark)
                exp_mark = [None] * nt
                pv_ready = [None] * nt

                def emit_S(j, nt=nt, kbuf=kbuf, qb=qb, h=h, qi=qi,
                           exp_mark=exp_mark, pv_ready=pv_ready):
                    bank = j % 2
                    em.waitm("tensor", bank_free[bank])
                    v = em.op("tensor", lambda e, j=j, bank=bank:
                              e.matmul(pb[bank][:, :],
                                       kbuf[:, j * 128:(j + 1) * 128], qb,
                                       start=True, stop=True), s_pe, 1)
                    sl = j % 3
                    if debug and h == 0 and qi == 0 and j == 0:
                        em.wait("vector", s_pe, v)
                        em.op("vector", lambda e, bank=bank:
                              e.tensor_copy(F1[:, 6144:6656],
                                            pb[bank][:, :]), s_dve, 1)
                        em.op("vector", lambda e, qb=qb:
                              e.tensor_copy(F1[:, 6656:7168], qb), s_dve, 1)
                        em.op("vector", lambda e, kbuf=kbuf:
                              e.tensor_copy(F1[:, 7168:7296],
                                            kbuf[:, 0:128]), s_dve, 1)
                        dsv3 = s_dve.v
                        em.wait("sync", s_dve, dsv3)
                        em.op("sync", lambda e: e.dma_start(
                            out=dbg["d_ps"][:, 0:512], in_=F1[:, 6144:6656]),
                            s_sts, 16)
                        em.op("sync", lambda e: e.dma_start(
                            out=dbg["d_ps"][:, 512:1024], in_=F1[:, 6656:7168]),
                            s_sts, 16)
                        dss = em.op("sync", lambda e: e.dma_start(
                            out=dbg["d_ps"][:, 1024:1152], in_=F1[:, 7168:7296]),
                            s_sts, 16)
                        em.wait("scalar", s_sts, dss)
                    em.waitm("scalar", pt_mark[sl])
                    em.wait("scalar", s_pe, v)
                    av = em.op("scalar", lambda e, bank=bank, sl=sl:
                               e.activation(ptsl[sl], pb[bank][:, :], AF.Exp,
                                            bias=cneg_sb[:, h:h + 1],
                                            scale=ISQHD), s_act, 1)
                    bank_free[bank] = (s_act, av)
                    exp_mark[j] = (s_act, av)
                    if j >= nt - 4:
                        dj = j - (nt - 4)
                        em.waitm("vector", exp_mark[j])
                        em.waitm("vector", ptm_mark[dj])
                        dvv = em.op("vector", lambda e, sl=sl, dj=dj:
                                    e.tensor_tensor(ptm[dj], ptsl[sl],
                                                    A2[:, dj * 512:(dj + 1) * 512],
                                                    AluOpType.mult), s_dve, 1)
                        pv_ready[j] = (s_dve, dvv)
                    else:
                        pv_ready[j] = exp_mark[j]

                def emit_PV(j, nt=nt, vbuf=vbuf, ao_bank=ao_bank,
                            pv_ready=pv_ready):
                    sl = j % 3
                    src = ptm[j - (nt - 4)] if j >= nt - 4 else ptsl[sl]
                    em.waitm("tensor", pv_ready[j])
                    em.op("tensor", lambda e, j=j, src=src, st=(j == 0),
                          sp=(j == nt - 1):
                          e.matmul(pb[ao_bank][:, :],
                                   vbuf[:, j * 128:(j + 1) * 128], src,
                                   start=st, stop=sp))
                    v = em.op("tensor", lambda e, src=src, st=(j == 0),
                              sp=(j == nt - 1):
                              e.matmul(pb[6][0:1, :], ones_col, src,
                                       start=st, stop=sp), s_pe, 1)
                    pt_mark[sl] = (s_pe, v)
                    if j >= nt - 4:
                        ptm_mark[j - (nt - 4)] = (s_pe, v)
                    return v

                last_pv = None
                for j in range(nt):
                    emit_S(j)
                    if j >= 2:
                        last_pv = emit_PV(j - 2)
                q_refill[qi % 2] = (s_pe, s_pe.v)
                last_pv = emit_PV(nt - 2)
                last_pv = emit_PV(nt - 1)
                em.wait("scalar", s_pe, last_pv)
                av = em.op("scalar", lambda e: e.copy(ssum_sb, pb[6][0:1, :]),
                           s_act, 1)
                sums_sc_mark = (s_act, av)
                if debug:
                    em.wait("sync", s_act, av)
                    em.op("sync", lambda e, h=h, qi=qi:
                          e.dma_start(out=dbg["d_sums"][h * 16 + qi:
                                                        h * 16 + qi + 1, :],
                                      in_=ssum_sb), s_sts, 16)
                em.wait("vector", s_act, av)
                dv = em.op("vector", lambda e: e.reciprocal(rs_sb, ssum_sb),
                           s_dve, 1)
                em.waitm("tensor", bank_free[7])
                em.wait("tensor", s_dve, dv)
                pv = em.op("tensor", lambda e: e.matmul(pb[7][:, :], ones_row_f,
                                                        rs_sb, start=True,
                                                        stop=True), s_pe, 1)
                em.wait("scalar", s_pe, pv)
                av = em.op("scalar", lambda e: e.copy(bc_sb, pb[7][:, :]),
                           s_act, 1)
                bank_free[7] = (s_act, av)
                sl2 = qi % 2
                em.waitm("vector", ao_stg_mark[sl2])
                em.wait("vector", s_act, av)
                dv = em.op("vector", lambda e, ao_bank=ao_bank, sl2=sl2:
                           e.tensor_tensor(aosl[sl2], pb[ao_bank][:, :], bc_sb,
                                           AluOpType.mult), s_dve, 1)
                ao_bank_mark[qi % 2] = (s_dve, dv)
                em.wait("sync", s_dve, dv)
                rblk = qi // 2
                stv = em.op("sync", lambda e, h=h, rblk=rblk, qi=qi, sl2=sl2:
                            e.dma_start(
                                out=ao_in[rblk * 512 + h * 128:
                                          rblk * 512 + (h + 1) * 128,
                                          (qi % 2) * 512:(qi % 2) * 512 + 512],
                                in_=aosl[sl2]), s_stg[sl2], 16)
                ao_stg_mark[sl2] = (s_stg[sl2], stv)
                if h < 3 and qi >= 8:
                    for _ in range(12):
                        emit_quant_piece()
            kv_free[h % 2] = (s_pe, s_pe.v)
            if h + 2 < HPC:
                kvbufs[h + 2] = emit_kv_load(h + 2)
        while qp_i < len(quant_pieces):
            emit_quant_piece()
        for wi, (loc, glob, cci) in enumerate([(wg_qloc, wg_q, 1), (wu_qloc, wu_q, 2),
                                               (wd_qloc, wd_q, 3)]):
            for m_ in quant_weight_done[wi]:
                em.waitm("gpsimd", m_)
            em.op("gpsimd", lambda e, loc=loc, glob=glob: e.collective_compute(
                "AllGather", AluOpType.bypass, replica_groups=[list(range(NCORES))],
                ins=[loc[:, :]], outs=[glob[:, :]]), s_cc[cci], 1)
        ao_done = [(s_stg[i], s_stg[i].v) for i in range(2)]
        p3_dve_end = s_dve.v

        dump_done = None
        if debug:
            for m_ in ao_done:
                em.waitm("sync", m_)
            for b in range(32):
                dump_done = em.op("sync", lambda e, b=b:
                                  e.dma_start(
                                      out=dbg["d_aoin"][b * 128:(b + 1) * 128, :],
                                      in_=ao_in[b * 128:(b + 1) * 128, :]),
                                  s_sts, 16)

        for m_ in ao_done:
            em.waitm("gpsimd", m_)
        if debug and dump_done is not None:
            em.wait("gpsimd", s_sts, dump_done)
        em.op("gpsimd", lambda e: e.collective_compute(
            "AllToAll", AluOpType.bypass, replica_groups=[list(range(NCORES))],
            ins=[ao_in[:, :]], outs=[ao_t[:, :]]), s_cc[4], 1)

        if debug:
            em.wait("sync", s_cc[4], 1)
            for ic in range(32):
                em.op("sync", lambda e, ic=ic:
                      e.dma_start(out=dbg["d_aot"][ic * 128:(ic + 1) * 128, :],
                                  in_=ao_t[ic * 128:(ic + 1) * 128, :]), s_sts, 16)

        em.wait("sync", s_cc[4], 1)
        em.waitm("sync", kv_free[0])
        em.waitm("sync", kv_free[1])
        for ic in range(32):
            em.op("sync", lambda e, ic=ic:
                  e.dma_start(out=A1[:, ic * 1024:(ic + 1) * 1024],
                              in_=ao_t[ic * 128:(ic + 1) * 128, :]), s_ldr, 16)
        attT_lds = s_ldr.v

        wo_slice_mark = [None, None]
        zstg = [F1[:, 0:512], F1[:, 512:1024]]
        zstg_mark = [None, None]
        bo_row_b = A1[0:1, 32768:36864]
        ones_row_b2 = A1[0:1, 36864:36992]
        bol = em.op("gpsimd", lambda e: e.dma_start(out=bo_row_b,
                                                    in_=bo_row[:, :]), s_ldg, 16)
        em.op("vector", lambda e: e.memset(ones_row_b2, 1.0), s_dve, 1)
        ones2_dve = s_dve.v
        xsl = [F1[:, 5120:5632], F1[:, 5632:6144]]
        xsl_mark = [None, None]
        xsl_ld = [None, None]
        for of in range(8):
            buf = A2 if of % 2 == 0 else A3
            if of < 2:
                em.wait("gpsimd", s_dve, p3_dve_end)
                for m_ in ao_done:
                    em.waitm("gpsimd", m_)
                em.waitm("gpsimd", kv_free[1])
            else:
                em.waitm("gpsimd", wo_slice_mark[of % 2])
            for ic in range(32):
                em.op("gpsimd", lambda e, buf=buf, ic=ic, of=of:
                      e.dma_start(out=buf[:, ic * 512:(ic + 1) * 512],
                                  in_=woT[of * D + ic * 128:of * D + (ic + 1) * 128,
                                          :]), s_ldx[of % 2], 16)
            em.wait("tensor", s_ldx[of % 2], s_ldx[of % 2].v)
            if of == 0:
                em.wait("tensor", s_ldr, attT_lds)
            for t in range(8):
                bank = t % 2
                xs2 = t % 2
                em.waitm("gpsimd", xsl_mark[xs2])
                xl = em.op("gpsimd", lambda e, t=t, of=of, xs2=xs2:
                           e.dma_start(out=xsl[xs2],
                                       in_=x_own[t * 128:(t + 1) * 128,
                                                 of * 512:(of + 1) * 512]),
                           s_ldz[xs2], 16)
                xsl_ld[xs2] = (s_ldz[xs2], xl)
                emits = []
                for ic in range(32):
                    emits.append(lambda e, ic=ic, t=t, buf=buf, bank=bank,
                                 st=(ic == 0):
                                 e.matmul(pb[bank][:, :],
                                          A1[:, ic * 1024 + t * 128:
                                               ic * 1024 + (t + 1) * 128],
                                          buf[:, ic * 512:(ic + 1) * 512],
                                          start=st, stop=False))
                emits.append(lambda e, bank=bank, of=of:
                             e.matmul(pb[bank][:, :], ones_row_b2,
                                      bo_row_b[0:1, of * 512:(of + 1) * 512],
                                      start=False, stop=True))
                if of == 0 and t == 0:
                    em.wait("tensor", s_ldg, bol)
                    em.wait("tensor", s_dve, ones2_dve)
                mk = mm_group(bank, emits)
                s2 = t % 2
                em.waitm("vector", mk)
                em.waitm("vector", zstg_mark[s2])
                em.waitm("vector", xsl_ld[xs2])
                av = em.op("vector", lambda e, bank=bank, s2=s2, xs2=xs2:
                           e.tensor_tensor(zstg[s2], pb[bank][:, :],
                                           xsl[xs2], AluOpType.add), s_dve, 1)
                bank_free[bank] = (s_dve, av)
                xsl_mark[xs2] = (s_dve, av)
                em.wait("sync", s_dve, av)
                stv = em.op("sync", lambda e, t=t, of=of, s2=s2:
                            e.dma_start(out=zd[t * 128:(t + 1) * 128,
                                               of * 512:(of + 1) * 512],
                                        in_=zstg[s2]), s_stg[s2], 16)
                zstg_mark[s2] = (s_stg[s2], stv)
            wo_slice_mark[of % 2] = (s_pe, s_pe.v)
        zd_done = [(s_stg[i], s_stg[i].v) for i in range(2)]

        if debug:
            for m_ in zd_done:
                em.waitm("sync", m_)
            for t in range(8):
                em.op("sync", lambda e, t=t:
                      e.dma_start(out=dbg["d_zd"][t * 128:(t + 1) * 128, :],
                                  in_=zd[t * 128:(t + 1) * 128, :]), s_sts, 16)

        mean_ap = CF[:, 208:209]
        var_ap = CF[:, 209:210]
        rstd_ap = CF[:, 210:211]
        ln_tmp = CF[:, 211:212]

        def emit_ln(r_reg, alt_reg, out32_reg, outbf_reg):
            v_ = em.op("vector", lambda e: e.tensor_reduce(ln_tmp, r_reg,
                                                           axis=AX,
                                                           op=AluOpType.add),
                       s_dve, 1)
            em.wait("vector", s_dve, v_)
            v_ = em.op("vector", lambda e: e.tensor_scalar(mean_ap, ln_tmp,
                                                           1.0 / D, None,
                                                           AluOpType.mult,
                                                           AluOpType.bypass),
                       s_dve, 1)
            em.wait("vector", s_dve, v_)
            v_ = em.op("vector", lambda e: e.tensor_scalar(alt_reg, r_reg,
                                                           mean_ap, None,
                                                           AluOpType.subtract,
                                                           AluOpType.bypass),
                       s_dve, 1)
            em.wait("vector", s_dve, v_)
            v_ = em.op("vector", lambda e: e.tensor_tensor(r_reg, alt_reg,
                                                           alt_reg,
                                                           AluOpType.mult),
                       s_dve, 1)
            em.wait("vector", s_dve, v_)
            v_ = em.op("vector", lambda e: e.tensor_reduce(ln_tmp, r_reg,
                                                           axis=AX,
                                                           op=AluOpType.add),
                       s_dve, 1)
            em.wait("vector", s_dve, v_)
            dv = em.op("vector", lambda e: e.tensor_scalar(var_ap, ln_tmp,
                                                           1.0 / D, EPS,
                                                           AluOpType.mult,
                                                           AluOpType.add),
                       s_dve, 1)
            em.wait("scalar", s_dve, dv)
            av_ = em.op("scalar", lambda e: e.activation(ln_tmp, var_ap, AF.Ln),
                        s_act, 1)
            em.wait("scalar", s_act, av_)
            av = em.op("scalar", lambda e: e.activation(rstd_ap, ln_tmp, AF.Exp,
                                                        scale=-0.5), s_act, 1)
            em.wait("vector", s_act, av)
            marks = []
            if outbf_reg is not None:
                mv = em.op("vector", lambda e:
                           e.tensor_scalar(outbf_reg, alt_reg, rstd_ap, None,
                                           AluOpType.mult, AluOpType.bypass),
                           s_dve, 1)
                marks.append(mv)
            if out32_reg is not None:
                mv = em.op("vector", lambda e:
                           e.tensor_scalar(out32_reg, alt_reg, rstd_ap, None,
                                           AluOpType.mult, AluOpType.bypass),
                           s_dve, 1)
                marks.append(mv)
            return (s_dve, marks[-1])

        xn_stg_mark = None
        for t in range(8):
            if t == 0:
                for m_ in zd_done:
                    em.waitm("sync", m_)
            if xn_stg_mark is not None:
                em.waitm("sync", xn_stg_mark)
            lv1 = em.op("sync", lambda e, t=t:
                        e.dma_start(out=F1[:, 0:4096],
                                    in_=zd[t * 128:(t + 1) * 128, :]), s_ldr, 16)
            em.wait("vector", s_ldr, lv1)
            if xn_stg_mark is not None:
                em.waitm("vector", xn_stg_mark)
            mk = emit_ln(F1[:, 0:4096], F1[:, 4096:8192],
                         F1[:, 0:4096], A2[:, 0:4096])
            em.waitm("sync", mk)
            st1 = em.op("sync", lambda e, t=t:
                        e.dma_start(out=xnbfd[t * 128:(t + 1) * 128, :],
                                    in_=A2[:, 0:4096]), s_stg[2], 16)
            st2 = em.op("sync", lambda e, t=t:
                        e.dma_start(out=xn32d[t * 128:(t + 1) * 128, :],
                                    in_=F1[:, 0:4096]), s_stg[2], 16)
            xn_stg_mark = (s_stg[2], st2)
        xn_done = (s_stg[2], s_stg[2].v)

        if debug:
            em.waitm("sync", xn_done)
            for t in range(8):
                lv = em.op("sync", lambda e, t=t:
                           e.dma_start(out=F1[:, 0:4096],
                                       in_=xn32d[t * 128:(t + 1) * 128, :]),
                           s_ldr, 16)
                em.wait("sync", s_ldr, lv)
                em.op("sync", lambda e, t=t:
                      e.dma_start(out=dbg["d_xn"][t * 128:(t + 1) * 128, :],
                                  in_=F1[:, 0:4096]), s_sts, 16)

        interT = A1[:, 0:22528]
        wd_slots = [A1[:, 22528:33792], A1[:, 33792:45056]]
        wg_slots = [A3[:, 0:8192], A3[:, 8192:16384]]

        em.wait("sync", s_cc[1], 1)
        em.wait("sync", s_cc[2], 1)
        em.wait("sync", s_cc[3], 1)

        xnT_mark = None
        inter_free = None
        gw_slot_mark = [None, None]
        wd_slot_mark = [None, None]
        dstg = [F1[:, 0:256], F1[:, 256:512]]
        dstg_mark = [None, None]
        z2a_slots = [F1[:, 1024:1280], F1[:, 1280:1536]]
        xns_slots = [F1[:, 1536:1792], F1[:, 1792:2048]]
        r2s = [F1[:, 2048:2304], F1[:, 2304:2560]]
        r2s_mark = [None, None]
        ih1_slot_mark = [None, None]
        gtmp = F1[:, 6144:6656]

        y_final = None
        for th in range(2):
            em.waitm("sync", xn_done)
            em.waitm("sync", xnT_mark)
            for fc in range(NFC):
                em.op("sync", lambda e, fc=fc, th=th:
                      e.dma_start(out=A2[:, fc * 512:(fc + 1) * 512],
                                  in_=xnbfd[th * 512:(th + 1) * 512,
                                            fc * 128:(fc + 1) * 128],
                                  transpose=True), s_ldm, 16)
            xnT_lds = s_ldm.v
            for ih in range(2):
                inter_dve = None
                for z in range(44):
                    ic = 44 * ih + z
                    sl = z % 2
                    em.waitm("sync", gw_slot_mark[sl])
                    em.op("sync", lambda e, ic=ic, sl=sl:
                          e.dma_start(out=wg_slots[sl][:, 0:4096],
                                      in_=wg_q[ic * 128:(ic + 1) * 128, :]),
                          s_ldw[sl], 16)
                    lv = em.op("sync", lambda e, ic=ic, sl=sl:
                               e.dma_start(out=wg_slots[sl][:, 4096:8192],
                                           in_=wu_q[ic * 128:(ic + 1) * 128, :]),
                               s_ldw[sl], 16)
                    pgb, pub = 2 * sl, 1 + 2 * sl
                    em.wait("tensor", s_ldw[sl], lv)
                    if z < 2:
                        em.wait("tensor", s_ldm, xnT_lds)
                    if z == 0:
                        em.waitm("tensor", inter_free)
                    emg = []
                    emu = []
                    for fc in range(NFC):
                        emg.append(lambda e, fc=fc, sl=sl, pgb=pgb, st=(fc == 0),
                                   sp=(fc == NFC - 1):
                                   e.matmul(pb[pgb][:, :],
                                            wg_slots[sl][:, fc * 128:(fc + 1) * 128],
                                            A2[:, fc * 512:(fc + 1) * 512],
                                            start=st, stop=sp))
                        emu.append(lambda e, fc=fc, sl=sl, pub=pub, st=(fc == 0),
                                   sp=(fc == NFC - 1):
                                   e.matmul(pb[pub][:, :],
                                            wg_slots[sl][:, 4096 + fc * 128:
                                                         4096 + (fc + 1) * 128],
                                            A2[:, fc * 512:(fc + 1) * 512],
                                            start=st, stop=sp))
                    mm_group(pgb, emg)
                    mk = mm_group(pub, emu)
                    gw_slot_mark[sl] = (s_pe, mk[1])
                    em.waitm("scalar", mk)
                    av = em.op("scalar", lambda e, pgb=pgb:
                               e.copy(gtmp, pb[pgb][:, :]), s_act, 1)
                    bank_free[pgb] = (s_act, av)
                    em.wait("vector", s_act, av)
                    dv = em.op("vector", lambda e, z=z, pub=pub:
                               e.tensor_tensor(interT[:, z * 512:(z + 1) * 512],
                                               pb[pub][:, :], gtmp,
                                               AluOpType.mult), s_dve, 1)
                    bank_free[pub] = (s_dve, dv)
                    inter_dve = dv
                em.wait("tensor", s_dve, inter_dve)
                if ih == 1:
                    em.waitm("sync", dstg_mark[0])
                    em.waitm("sync", dstg_mark[1])
                rr0 = 4 * ih
                for b in range(16):
                    sl = b % 2
                    em.waitm("sync", wd_slot_mark[sl])
                    lv = None
                    for ri in range(4):
                        r = rr0 + ri
                        lv = em.op("sync", lambda e, r=r, b=b, sl=sl, ri=ri:
                                   e.dma_start(
                                       out=wd_slots[sl][:, ri * 2816:
                                                        (ri + 1) * 2816],
                                       in_=wd_q[r * 2048 + b * 128:
                                                r * 2048 + (b + 1) * 128, :]),
                                   s_ldd[sl], 16)
                    em.wait("tensor", s_ldd[sl], lv)
                    for t in range(4):
                        bank = 4 + (b * 4 + t) % 2
                        emits = []
                        for z in range(44):
                            ri, icl = z // 11, z % 11
                            emits.append(
                                lambda e, z=z, t=t, ri=ri, icl=icl, sl=sl,
                                bank=bank, st=(z == 0), sp=(z == 43):
                                e.matmul(pb[bank][:, 0:256],
                                         interT[:, z * 512 + t * 128:
                                                z * 512 + (t + 1) * 128],
                                         wd_slots[sl][:, ri * 2816 + icl * 256:
                                                      ri * 2816 +
                                                      (icl + 1) * 256],
                                         start=st, stop=sp))
                        mk = mm_group(bank, emits)
                        trow = th * 512 + t * 128
                        if ih == 0:
                            s2 = (b * 4 + t) % 2
                            em.waitm("vector", dstg_mark[s2])
                            em.waitm("vector", mk)
                            dv = em.op("vector", lambda e, bank=bank, s2=s2:
                                       e.tensor_scalar(dstg[s2], pb[bank][:, 0:256],
                                                       stot_ap, None,
                                                       AluOpType.mult,
                                                       AluOpType.bypass),
                                       s_dve, 1)
                            bank_free[bank] = (s_dve, dv)
                            em.wait("sync", s_dve, dv)
                            stv = em.op("sync", lambda e, trow=trow, b=b, s2=s2:
                                        e.dma_start(
                                            out=z2p[trow:trow + 128,
                                                    b * 256:(b + 1) * 256],
                                            in_=dstg[s2]), s_stg[s2], 16)
                            dstg_mark[s2] = (s_stg[s2], stv)
                        else:
                            s2 = (b * 4 + t) % 2
                            em.waitm("sync", ih1_slot_mark[s2])
                            lza = em.op("sync", lambda e, trow=trow, b=b, s2=s2:
                                        e.dma_start(out=z2a_slots[s2],
                                                    in_=z2p[trow:trow + 128,
                                                            b * 256:
                                                            (b + 1) * 256]),
                                        s_ldz[s2], 16)
                            lxn = em.op("sync", lambda e, trow=trow, b=b, s2=s2:
                                        e.dma_start(out=xns_slots[s2],
                                                    in_=xn32d[trow:trow + 128,
                                                              b * 256:
                                                              (b + 1) * 256]),
                                        s_ldz[s2], 16)
                            em.wait("vector", s_ldz[s2], lxn)
                            em.waitm("vector", mk)
                            em.waitm("vector", r2s_mark[s2])
                            v_ = em.op("vector", lambda e, bank=bank, s2=s2:
                                       e.scalar_tensor_tensor(dstg[s2],
                                                              pb[bank][:, 0:256],
                                                              stot_ap,
                                                              z2a_slots[s2],
                                                              AluOpType.mult,
                                                              AluOpType.add),
                                       s_dve, 1)
                            em.wait("vector", s_dve, v_)
                            dv = em.op("vector", lambda e, s2=s2:
                                       e.tensor_add(r2s[s2], dstg[s2],
                                                    xns_slots[s2]), s_dve, 1)
                            bank_free[bank] = (s_dve, dv)
                            ih1_slot_mark[s2] = (s_dve, dv)
                            em.wait("sync", s_dve, dv)
                            stv = em.op("sync", lambda e, trow=trow, b=b, s2=s2:
                                        e.dma_start(
                                            out=r2d[trow:trow + 128,
                                                    b * 256:(b + 1) * 256],
                                            in_=r2s[s2]), s_stg[s2], 16)
                            r2s_mark[s2] = (s_stg[s2], stv)
                    wd_slot_mark[sl] = (s_pe, s_pe.v)
                inter_free = (s_pe, s_pe.v)
            r2_done = [(s_stg[i], s_stg[i].v) for i in range(2)]
            for t in range(4):
                trow = th * 512 + t * 128
                if t == 0:
                    for m_ in r2_done:
                        em.waitm("sync", m_)
                if y_final is not None:
                    em.wait("sync", s_sts, y_final)
                lv = em.op("sync", lambda e, trow=trow:
                           e.dma_start(out=F1[:, 4096:8192],
                                       in_=r2d[trow:trow + 128, :]), s_ldr, 16)
                em.wait("vector", s_ldr, lv)
                mk = emit_ln(F1[:, 4096:8192], F1[:, 0:4096], F1[:, 4096:8192],
                             None)
                em.waitm("sync", mk)
                y_final = em.op("sync", lambda e, trow=trow:
                                e.dma_start(out=y[trow:trow + 128, :],
                                            in_=F1[:, 4096:8192]), s_sts, 16)
            xnT_mark = (s_pe, s_pe.v)

        if debug:
            em.op("sync", lambda e: e.dma_start(out=dbg["d_can"][:, 512:1024],
                                                in_=ao_in[2048:2560, 0:512]),
                  s_sts, 16)
            y_final = s_sts.v
        em.wait("sync", s_sts, y_final)

        with allow_ici_all_to_all(), nc.Block() as block:
            @block.sync
            def _(e):
                for fn in em.tape["sync"]:
                    fn(e)

            @block.scalar
            def _(e):
                for fn in em.tape["scalar"]:
                    fn(e)

            @block.vector
            def _(e):
                for fn in em.tape["vector"]:
                    fn(e)

            @block.tensor
            def _(e):
                for fn in em.tape["tensor"]:
                    fn(e)

            @block.gpsimd
            def _(e):
                for fn in em.tape["gpsimd"]:
                    fn(e)

    return nc


_NC_CACHE = {}


def _get_nc(debug=False):
    if debug not in _NC_CACHE:
        _NC_CACHE[debug] = build_program(debug=debug)
    return _NC_CACHE[debug]


def make_in_maps(L0_attn_x, L0_attn_Wq, L0_attn_bq, L0_attn_Wk, L0_attn_bk,
                 L0_attn_Wv, L0_attn_bv, L0_attn_Wo, L0_attn_bo,
                 L0_ffn_w_gate, L0_ffn_w_up, L0_ffn_w_down):
    f = np.float32
    x = np.asarray(L0_attn_x, f).reshape(L, D)
    xT = np.ascontiguousarray(
        x.T.reshape(D, NWIN, 512).transpose(1, 0, 2).reshape(NWIN * D, 512))
    Wq = np.asarray(L0_attn_Wq, f)
    Wk = np.asarray(L0_attn_Wk, f)
    Wv = np.asarray(L0_attn_Wv, f)
    Wo = np.asarray(L0_attn_Wo, f)
    bq = np.asarray(L0_attn_bq, f)
    bk = np.asarray(L0_attn_bk, f)
    bv = np.asarray(L0_attn_bv, f)
    bo = np.asarray(L0_attn_bo, f)
    wg = np.asarray(L0_ffn_w_gate, f)
    wu = np.asarray(L0_ffn_w_up, f)
    wd = np.asarray(L0_ffn_w_down, f)

    woT = np.ascontiguousarray(
        Wo.T.reshape(D, 8, 512).transpose(1, 0, 2).reshape(8 * D, 512))
    bo_row = np.ascontiguousarray(bo.reshape(1, D))

    wgp = np.zeros((IPAD, D), f)
    wgp[:I_TRUE] = wg
    wup = np.zeros((IPAD, D), f)
    wup[:I_TRUE] = wu
    wdTp = np.zeros((IPAD, D), f)
    wdTp[:I_TRUE] = wd.T

    mk = np.zeros((4, 128, 512), np.float32)
    kk = np.arange(128)[:, None]
    qq = np.arange(512)[None, :]
    for dj in range(4):
        mk[dj] = (128 * dj + kk <= qq).astype(np.float32)
    masks = mk.reshape(512, 512).astype(ml_dtypes.bfloat16)

    in_maps = []
    for c in range(NCORES):
        hs = c * HPC * HD
        wqT_c = np.ascontiguousarray(Wq[hs:hs + 512, :].T)
        wkT_c = np.ascontiguousarray(Wk[hs:hs + 512, :].T)
        wvT_c = np.ascontiguousarray(Wv[hs:hs + 512, :].T)
        bqk_c = np.zeros((128, 8), f)
        for h in range(HPC):
            bqk_c[:, h] = bq[hs + h * 128: hs + (h + 1) * 128]
            bqk_c[:, 4 + h] = bk[hs + h * 128: hs + (h + 1) * 128]
        bv_c = np.ascontiguousarray(bv[hs:hs + 512].reshape(1, 512))
        sh = wgp[c * ISH:(c + 1) * ISH, :]
        wgT_c = np.ascontiguousarray(
            sh.T.reshape(NFC, 128, NICL, 128).transpose(2, 1, 0, 3)
            .reshape(ISH, D))
        sh = wup[c * ISH:(c + 1) * ISH, :]
        wuT_c = np.ascontiguousarray(
            sh.T.reshape(NFC, 128, NICL, 128).transpose(2, 1, 0, 3)
            .reshape(ISH, D))
        sh = wdTp[c * ISH:(c + 1) * ISH, :]
        wdT_c = np.ascontiguousarray(
            sh.reshape(NICL, 128, 16, 256).transpose(2, 1, 0, 3)
            .reshape(2048, 2816))
        in_maps.append({
            "xT": xT,
            "x_own": np.ascontiguousarray(x[c * TPC:(c + 1) * TPC, :]),
            "wqT": wqT_c, "wkT": wkT_c, "wvT": wvT_c,
            "bqk": bqk_c, "bv_row": bv_c,
            "woT": woT, "bo_row": bo_row,
            "wgT_sh": wgT_c, "wuT_sh": wuT_c, "wdT_sh": wdT_c,
            "masks": masks,
        })
    return in_maps


def kernel(**inputs):
    in_maps = make_in_maps(**inputs)
    nc = _get_nc(debug=False)
    res = run_bass_kernel_spmd(nc, in_maps, core_ids=list(range(NCORES)))
    out = np.concatenate([np.asarray(res.results[c]["y"]) for c in range(NCORES)],
                         axis=0)
    return out.reshape(1, L, D).astype(np.float32)
